# version 02
# baseline (speedup 1.0000x reference)
import contextlib
import numpy as np
import ml_dtypes
import concourse.bass as bass
import concourse.mybir as mybir
from concourse.bass_utils import run_bass_kernel_spmd

F32 = mybir.dt.float32
BF16 = mybir.dt.bfloat16
AF = mybir.ActivationFunctionType
ALU = mybir.AluOpType
AX = mybir.AxisListType
bfloat16 = ml_dtypes.bfloat16

D = 2048
T = 2048
INTER = 8192
DC = 16
IC = 64
NT = 4
SEG = DC * 512
RMS_EPS = float(np.finfo(np.float32).eps)
LN_EPS = 1e-5
ISQ = 1.0 / float(np.sqrt(128.0))
S1 = 1.0 / (64.0 * float(np.sqrt(2048.0)))
S2 = 1.0 / (64.0 * float(np.sqrt(8192.0)))
ENGINES = ("sync", "gpsimd", "tensor", "vector", "scalar")



class Sem:
    __slots__ = ("h", "n", "name")

    def __init__(self, h, name):
        self.h = h
        self.n = 0
        self.name = name


class Em:

    def __init__(self, nc, stack):
        self.nc = nc
        self.stack = stack
        self.tape = {e: [] for e in ENGINES}
        self.waited = {e: {} for e in ENGINES}

    def sem(self, name):
        h = self.stack.enter_context(self.nc.semaphore(name))
        return Sem(h, name)

    def wait(self, eng, sem, val):
        if sem is None or val is None or val <= 0:
            return
        if self.waited[eng].get(sem.name, 0) >= val:
            return
        self.waited[eng][sem.name] = val
        h = sem.h
        self.tape[eng].append(lambda e, h=h, v=val: e.wait_ge(h, v))

    def wait_evt(self, eng, evt):
        if evt is not None:
            self.wait(eng, evt[0], evt[1])

    def op(self, eng, meth, *args, inc=None, **kw):
        incs = []
        evt = None
        if inc:
            for s, k in inc:
                s.n += k
                incs.append((s.h, k))
                if evt is None:
                    evt = (s, s.n)

        def fn(e, meth=meth, args=args, kw=kw, incs=incs):
            ins = getattr(e, meth)(*args, **kw)
            for h, k in incs:
                ins.then_inc(h, k)
            return ins

        self.tape[eng].append(fn)
        return evt

    def replay(self):
        with self.nc.Block() as blk:
            @blk.sync
            def _(e):
                for f in self.tape["sync"]:
                    f(e)

            @blk.gpsimd
            def _(e):
                for f in self.tape["gpsimd"]:
                    f(e)

            @blk.tensor
            def _(e):
                for f in self.tape["tensor"]:
                    f(e)

            @blk.vector
            def _(e):
                for f in self.tape["vector"]:
                    f(e)

            @blk.scalar
            def _(e):
                for f in self.tape["scalar"]:
                    f(e)


class Banks:
    def __init__(self, tiles):
        self.tiles = tiles
        self.free = [None] * len(tiles)
        self.idx = 0

    def acquire(self):
        i = self.idx % len(self.tiles)
        self.idx += 1
        return i, self.tiles[i], self.free[i]

    def release(self, i, evt):
        self.free[i] = evt


class Ctx:
    pass


def barrier(c):
    em = c.em
    for eng in ENGINES:
        for s in (c.dG, c.dS, c.tM, c.vV, c.sA):
            em.wait(eng, s, s.n)
    c.pstat_free = [None, None]
    c.acc.free = [None] * len(c.acc.tiles)
    c.ptr.free = [None] * len(c.ptr.tiles)


def make_ctx(nc, stack):
    em = Em(nc, stack)
    c = Ctx()
    c.nc = nc
    c.em = em
    c.dG = em.sem("dG")
    c.dS = em.sem("dS")
    c.tM = em.sem("tM")
    c.vV = em.sem("vV")
    c.sA = em.sem("sA")
    c.acc = Banks([stack.enter_context(nc.psum_tensor(f"acc{i}", [128, 512], F32))
                   for i in range(4)])
    c.ptr = Banks([stack.enter_context(nc.psum_tensor(f"ptr{i}", [128, 1024], BF16))
                   for i in range(2)])
    c.pstat = [stack.enter_context(nc.psum_tensor(f"pst{i}", [1, 512], F32))
               for i in range(2)]
    c.pstat_free = [None, None]
    c.idn = stack.enter_context(nc.sbuf_tensor("idn", [128, 128], BF16))
    c.ones_col = stack.enter_context(nc.sbuf_tensor("ones_col", [128, 1], BF16))
    c.ones_row = stack.enter_context(nc.sbuf_tensor("ones_row", [1, 128], BF16))
    c.epsc = stack.enter_context(nc.sbuf_tensor("epsc", [128, 2], F32))
    eps_d = nc.dram_tensor("EPSC", [128, 2], F32, kind="ExternalInput")
    idn_d = nc.dram_tensor("IDN", [128, 128], BF16, kind="ExternalInput")
    onc_d = nc.dram_tensor("ONESC", [128, 1], BF16, kind="ExternalInput")
    onr_d = nc.dram_tensor("ONESR", [1, 128], BF16, kind="ExternalInput")
    em.op("sync", "dma_start", c.epsc[:, :], eps_d[:, :], inc=[(c.dS, 16)])
    em.op("sync", "dma_start", c.idn[:, :], idn_d[:, :], inc=[(c.dS, 16)])
    em.op("sync", "dma_start", c.ones_col[:, :], onc_d[:, :], inc=[(c.dS, 16)])
    c.const_evt = em.op("sync", "dma_start", c.ones_row[:, :], onr_d[:, :],
                        inc=[(c.dS, 16)])
    return c


def const_inputs():
    return {
        "EPSC": np.tile(np.array([[RMS_EPS, LN_EPS]], np.float32), (128, 1)),
        "IDN": np.eye(128, dtype=np.float32).astype(bfloat16),
        "ONESC": np.ones((128, 1), np.float32).astype(bfloat16),
        "ONESR": np.ones((1, 128), np.float32).astype(bfloat16),
    }



def emit_wt_load(c, W_dram, wt_sb, wt_off, natbufs, nat_free, n_oslabs,
                 d_free, wt_free_evt=None):
    em = c.em
    nk = d_free // 128
    slab_ready = []
    for s in range(n_oslabs):
        nbi = s % len(natbufs)
        nb = natbufs[nbi]
        em.wait_evt("gpsimd", nat_free[nbi])
        ld = em.op("gpsimd", "dma_start", nb[:, 0:d_free],
                   W_dram[s * 128:(s + 1) * 128, 0:d_free], inc=[(c.dG, 16)])
        last_evac = None
        tev = None
        for g in range(nk // 4):
            bi, bt, bfree = c.ptr.acquire()
            em.wait_evt("tensor", ld)
            em.wait_evt("tensor", bfree)
            for i in range(4):
                k = g * 4 + i
                inc = [(c.tM, 1)] if i == 3 else None
                tev = em.op("tensor", "transpose", bt[:, i * 128:(i + 1) * 128],
                            nb[:, k * 128:(k + 1) * 128], c.idn[:, :], inc=inc)
            em.wait_evt("vector", tev)
            em.wait_evt("vector", wt_free_evt)
            off = wt_off + s * d_free + g * 4 * 128
            last_evac = em.op("vector", "tensor_copy",
                              wt_sb[:, off:off + 512], bt[:, 0:512],
                              inc=[(c.vV, 1)])
            c.ptr.release(bi, last_evac)
        nat_free[nbi] = tev
        slab_ready.append(last_evac)
    return slab_ready



def emit_stats_mms(c, y_ap, y_evt, sq_buf, sq_free, first, last):
    em = c.em
    em.wait_evt("scalar", y_evt)
    em.wait_evt("scalar", sq_free)
    sqe = em.op("scalar", "activation", sq_buf, y_ap, AF.Square,
                inc=[(c.sA, 1)])
    if first:
        em.wait_evt("tensor", c.pstat_free[0])
        em.wait_evt("tensor", c.pstat_free[1])
    em.wait_evt("tensor", y_evt)
    em.op("tensor", "matmul", c.pstat[0][:, :], c.ones_col[:, :], y_ap,
          start=first, stop=last)
    em.wait_evt("tensor", sqe)
    ev = em.op("tensor", "matmul", c.pstat[1][:, :], c.ones_col[:, :],
               sq_buf, start=first, stop=last, inc=[(c.tM, 1)])
    return ev, sqe


def emit_ln_finalize(c, s, stats_evt):
    em = c.em
    em.wait_evt("vector", stats_evt)
    em.op("vector", "tensor_scalar", s["m"], c.pstat[0][:, :], 1.0 / D, None,
          ALU.mult)
    em.op("vector", "tensor_tensor", s["t0"], s["m"], s["m"], ALU.mult)
    ev_var = em.op("vector", "scalar_tensor_tensor", s["t1"], c.pstat[1][:, :],
                   1.0 / D, s["t0"], ALU.mult, ALU.subtract, inc=[(c.vV, 1)])
    c.pstat_free[0] = ev_var
    c.pstat_free[1] = ev_var
    em.wait_evt("scalar", ev_var)
    em.op("scalar", "activation", s["t0"], s["t1"], AF.Ln,
          bias=c.epsc[0:1, 1:2])
    ev_r = em.op("scalar", "activation", s["t1"], s["t0"], AF.Exp,
                 scale=-0.5, inc=[(c.sA, 1)])
    em.wait_evt("vector", ev_r)
    evA = em.op("vector", "tensor_copy", s["row_a"], s["t1"], inc=[(c.vV, 1)])
    em.op("vector", "scalar_tensor_tensor", s["t0"], s["m"], -1.0, s["t1"],
          ALU.mult, ALU.mult)
    evB = em.op("vector", "tensor_copy", s["row_b"], s["t0"], inc=[(c.vV, 1)])
    iA, Ab, fA = c.acc.acquire()
    em.wait_evt("tensor", fA)
    em.wait_evt("tensor", evA)
    evtA = em.op("tensor", "matmul", Ab[:, :], c.ones_row[:, :], s["row_a"],
                 start=True, stop=True, inc=[(c.tM, 1)])
    iB, Bb, fB = c.acc.acquire()
    em.wait_evt("tensor", fB)
    em.wait_evt("tensor", evB)
    evtB = em.op("tensor", "matmul", Bb[:, :], c.ones_row[:, :], s["row_b"],
                 start=True, stop=True, inc=[(c.tM, 1)])
    return (iA, Ab, evtA), (iB, Bb, evtB)


def emit_ln_apply(c, g, Y, out_writer, A_t, B_t):
    em = c.em
    iA, Ab, evtA = A_t
    iB, Bb, evtB = B_t
    pe = None
    for o in range(DC):
        em.wait_evt("vector", evtA)
        em.wait_evt("vector", evtB)
        si = o % 2
        em.wait_evt("vector", g.scr_free[si])
        em.op("vector", "tensor_tensor", g.scr[si][:, :],
              Y[:, o * 512:(o + 1) * 512], Ab[:, :], ALU.mult)
        pe = out_writer(o, g.scr[si])
        g.scr_free[si] = pe
    c.acc.release(iA, pe)
    c.acc.release(iB, pe)
    return pe


class G:
    pass


def alloc_arenas(nc, stack):
    g = G()
    g.G1 = stack.enter_context(nc.sbuf_tensor("G1", [128, DC * D], BF16))
    g.G2 = stack.enter_context(nc.sbuf_tensor("G2", [128, DC * D], BF16))
    g.G3 = stack.enter_context(nc.sbuf_tensor("G3", [128, 2 * SEG], BF16))
    g.G4 = stack.enter_context(nc.sbuf_tensor("G4", [128, SEG], BF16))
    g.FA = stack.enter_context(nc.sbuf_tensor("FA", [128, 5248], F32))
    g.BA = stack.enter_context(nc.sbuf_tensor("BA", [128, 4480], BF16))
    g.scr = [g.FA[:, 0:512], g.FA[:, 512:1024]]
    g.ln = {
        "m": g.FA[0:1, 1024:1536], "t0": g.FA[0:1, 1536:2048],
        "t1": g.FA[0:1, 2048:2560],
    }
    g.S_sb = g.FA[:, 2560:4608]
    g.rowb = g.FA[:, 4608:4640]
    g.vecsB = g.FA[:, 4640:5152]
    g.vecsD = g.FA[:, 5152:5248]
    g.piece = [g.BA[:, 0:512], g.BA[:, 512:1024]]
    g.ln["row_a"] = g.BA[0:1, 1024:1536]
    g.ln["row_b"] = g.BA[0:1, 1536:2048]
    g.h128 = g.BA[:, 2048:2176]
    g.amask = g.BA[:, 2176:2304]
    g.og = [g.BA[:, 2304:2816], g.BA[:, 2816:3328]]
    g.x3p = [g.BA[:, 3328:3840], g.BA[:, 3840:4352]]
    g.lmask_off = None
    g.scr_free = [None, None]
    g.piece_free = [None, None]
    return g



def emit_block_A(c, g, dr):
    em = c.em
    nc = c.nc
    xT, Wq, Wk, Wv, Wg, Wp, LMASK, x1_out = (
        dr["xT"], dr["Wq"], dr["Wk"], dr["Wv"], dr["Wg"], dr["Wp"],
        dr["LMASK"], dr["x1"])
    q_scr, k_scr, g_scr, og_scr, r_scr = (
        dr["q_scr"], dr["k_scr"], dr["g_scr"], dr["og_scr"], dr["r_scr"])

    AR_W = g.G1
    AR_V = g.G2
    xblk = [g.G3[:, 0:SEG], g.G3[:, SEG:2 * SEG]]
    r_bc = g.G4[:, 0:2048]
    nat = [g.G4[:, 2048 + i * 2048:2048 + (i + 1) * 2048] for i in range(2)]
    nat3 = [g.G2[:, 512 + i * 2048:512 + (i + 1) * 2048] for i in range(2)]
    r_tok = g.FA[:, 4640:4656]
    sq = g.G2[:, 0:512]
    scr = g.scr
    piece = g.piece
    ln_sc = g.ln
    lmask = AR_W[:, 3 * SEG:3 * SEG + 2048]

    xblk_free = [None, None]
    nat_free = [None, None]
    scr_free = g.scr_free
    piece_free = g.piece_free

    xT_re = xT[:, :].rearrange("(c p) n -> p c n", p=128)

    def load_xblk(nt, slot):
        em.wait_evt("gpsimd", xblk_free[slot])
        src = xT_re[:, :, nt * 512:(nt + 1) * 512]
        dst = xblk[slot].rearrange("p (c n) -> p c n", c=DC)
        return em.op("gpsimd", "dma_start", dst, src, inc=[(c.dG, 16)])

    em.wait_evt("tensor", c.const_evt)
    for nt in range(NT):
        lev = load_xblk(nt, nt % 2)
        sqe = None
        for ch in range(DC):
            ya = xblk[nt % 2][:, ch * 512:(ch + 1) * 512]
            em.wait_evt("scalar", lev)
            em.wait_evt("scalar", sqe)
            sqe = em.op("scalar", "activation", sq, ya, AF.Square,
                        inc=[(c.sA, 1)])
            if ch == 0:
                em.wait_evt("tensor", c.pstat_free[1])
            em.wait_evt("tensor", sqe)
            sev = em.op("tensor", "matmul", c.pstat[1][:, :], c.ones_col[:, :],
                        sq, start=(ch == 0), stop=(ch == DC - 1),
                        inc=[(c.tM, 1)])
        xblk_free[nt % 2] = sqe
        em.wait_evt("vector", sev)
        ev_var = em.op("vector", "tensor_scalar", ln_sc["t1"],
                       c.pstat[1][:, :], 1.0 / D, None, ALU.mult,
                       inc=[(c.vV, 1)])
        c.pstat_free[1] = ev_var
        em.wait_evt("scalar", ev_var)
        em.op("scalar", "activation", ln_sc["t0"], ln_sc["t1"], AF.Ln,
              bias=c.epsc[0:1, 0:1])
        ev_r = em.op("scalar", "activation", ln_sc["t1"], ln_sc["t0"], AF.Exp,
                     scale=-0.5, inc=[(c.sA, 1)])
        em.wait_evt("vector", ev_r)
        evA = em.op("vector", "tensor_copy", ln_sc["row_a"], ln_sc["t1"],
                    inc=[(c.vV, 1)])
        em.wait_evt("sync", ev_r)
        em.op("sync", "dma_start", r_scr[0:1, nt * 512:(nt + 1) * 512],
              ln_sc["t1"], inc=[(c.dS, 16)])
        ia, tA, fA = c.acc.acquire()
        em.wait_evt("tensor", fA)
        em.wait_evt("tensor", evA)
        bev = em.op("tensor", "matmul", tA[:, :], c.ones_row[:, :],
                    ln_sc["row_a"], start=True, stop=True, inc=[(c.tM, 1)])
        em.wait_evt("vector", bev)
        cev = em.op("vector", "tensor_copy", r_bc[:, nt * 512:(nt + 1) * 512],
                    tA[:, :], inc=[(c.vV, 1)])
        c.acc.release(ia, cev)
    r_rows_done = (c.dS, c.dS.n)
    em.wait("gpsimd", *r_rows_done)
    rt_src = r_scr[:, :].rearrange("one (cc p) -> p (one cc)", p=128)
    rt_ev = em.op("gpsimd", "dma_start", r_tok, rt_src, inc=[(c.dG, 16)],
                  allow_slow_non_contiguous=True)

    def proj_standard(wt_view, slab_ready, spill, kind):
        pi = [0]
        mme = None
        for nt in range(NT):
            lev = load_xblk(nt, nt % 2)
            for o in range(DC):
                ia, acc, fr = c.acc.acquire()
                em.wait_evt("tensor", fr)
                em.wait_evt("tensor", slab_ready[o])
                em.wait_evt("tensor", lev)
                for k in range(DC):
                    mme = em.op(
                        "tensor", "matmul", acc[:, :],
                        wt_view[:, o * D + k * 128:o * D + (k + 1) * 128],
                        xblk[nt % 2][:, k * 512:(k + 1) * 512],
                        start=(k == 0), stop=(k == DC - 1),
                        inc=([(c.tM, 1)] if k == DC - 1 else None))
                em.wait_evt("vector", mme)
                si = o % 2
                s0 = scr[si]
                em.wait_evt("vector", scr_free[si])
                rv = em.op("vector", "tensor_tensor", s0[:, :], acc[:, :],
                           r_bc[:, nt * 512:(nt + 1) * 512], ALU.mult,
                           inc=[(c.vV, 1)])
                c.acc.release(ia, rv)
                p = pi[0] % 2
                pc = piece[p]
                if kind == "silu":
                    em.wait_evt("scalar", rv)
                    em.wait_evt("scalar", piece_free[p])
                    pe = em.op("scalar", "activation", pc[:, :], s0[:, :],
                               AF.Silu, inc=[(c.sA, 1)])
                    scr_free[si] = pe
                else:
                    sit = (o + 1) % 2
                    em.wait_evt("scalar", rv)
                    em.wait_evt("scalar", scr_free[sit])
                    th = em.op("scalar", "activation", scr[sit][:, :],
                               s0[:, :], AF.Tanh, scale=0.5, inc=[(c.sA, 1)])
                    scr_free[si] = th
                    em.wait_evt("vector", th)
                    em.wait_evt("vector", piece_free[p])
                    pe = em.op("vector", "tensor_scalar", pc[:, :],
                               scr[sit][:, :], 0.5, 0.5, ALU.mult, ALU.add,
                               inc=[(c.vV, 1)])
                    scr_free[sit] = pe
                em.wait_evt("sync", pe)
                de = em.op("sync", "dma_start",
                           spill[nt * D + o * 128:nt * D + (o + 1) * 128, :],
                           pc[:, :], inc=[(c.dS, 16)])
                piece_free[p] = de
                pi[0] += 1
            xblk_free[nt % 2] = mme
        return mme

    wt_free = None
    sr = emit_wt_load(c, Wq, AR_W, 0, nat, nat_free, DC, D, wt_free)
    wt_free = proj_standard(AR_W, sr, q_scr, "silu")
    sr = emit_wt_load(c, Wk, AR_W, 0, nat, nat_free, DC, D, wt_free)
    wt_free = proj_standard(AR_W, sr, k_scr, "silu")
    sr = emit_wt_load(c, Wg, AR_W, 0, nat, nat_free, DC, D, wt_free)
    wt_free = proj_standard(AR_W, sr, g_scr, "sigmoid")
    sr = emit_wt_load(c, Wv, AR_W, 0, nat, nat_free, DC, D, wt_free)

    em.wait_evt("vector", rt_ev)
    wtv_re = AR_W[:, :].rearrange("p (s x) -> p s x", s=DC)
    for m in range(4):
        for nb in range(NT):
            lev = load_xblk(nb, nb % 2)
            for ns in range(4):
                nch = nb * 4 + ns
                ia, acc, fr = c.acc.acquire()
                em.wait_evt("tensor", fr)
                em.wait_evt("tensor", sr[4 * m + 3])
                em.wait_evt("tensor", lev)
                for k in range(DC):
                    rhs = wtv_re[:, 4 * m:4 * m + 4, k * 128:(k + 1) * 128]
                    mme = em.op(
                        "tensor", "matmul", acc[:, :],
                        xblk[nb % 2][:, k * 512 + ns * 128:k * 512 + (ns + 1) * 128],
                        rhs, start=(k == 0), stop=(k == DC - 1),
                        inc=([(c.tM, 1)] if k == DC - 1 else None))
                em.wait_evt("vector", mme)
                si = ns % 2
                em.wait_evt("vector", scr_free[si])
                rv = em.op("vector", "tensor_scalar", scr[si][:, :], acc[:, :],
                           r_tok[:, nch:nch + 1], None, ALU.mult,
                           inc=[(c.vV, 1)])
                c.acc.release(ia, rv)
                em.wait_evt("scalar", rv)
                ve = em.op("scalar", "activation",
                           AR_V[:, nch * D + m * 512:nch * D + (m + 1) * 512],
                           scr[si][:, :], AF.Silu, inc=[(c.sA, 1)])
                scr_free[si] = ve
            xblk_free[nb % 2] = mme
    V_ready = (c.sA, c.sA.n)
    a1_t_done = (c.tM, c.tM.n)
    a1_v_done = (c.vV, c.vV.n)

    lm_dst = lmask.rearrange("p (m n) -> p m n", m=4)
    lm_src = LMASK[:, :].rearrange("(m p) n -> p m n", p=128)
    em.wait_evt("sync", a1_t_done)
    lm_ev = em.op("sync", "dma_start", lm_dst, lm_src, inc=[(c.dS, 16)])
    a1_spill_done = (c.dS, c.dS.n)

    kblk = [AR_W[:, 0:SEG], AR_W[:, SEG:2 * SEG]]
    S_sb = AR_W[:, 2 * SEG:3 * SEG]
    qblk = xblk
    gblk = g.G4[:, 0:SEG]
    og_sb = g.og

    def load_blk(scr_d, dst, t, ev_w, eng, sem):
        em.wait_evt(eng, ev_w)
        src = scr_d[t * D:(t + 1) * D, :].rearrange("(c p) n -> p c n", p=128)
        dst3 = dst.rearrange("p (c n) -> p c n", c=DC)
        return em.op(eng, "dma_start", dst3, src, inc=[(sem, 16)])

    kblk_free = [None, None]
    gblk_free = None
    og_free = [None, None]
    qblk_free = [xblk_free[0], xblk_free[1]]
    Ssb_free = None
    ogi = 0
    em.wait_evt("gpsimd", a1_t_done)
    em.wait_evt("gpsimd", a1_v_done)
    for t in range(4):
        em.wait("sync", *a1_spill_done)
        em.wait("gpsimd", *a1_spill_done)
        qlev = load_blk(q_scr, qblk[t % 2], t, qblk_free[t % 2], "sync", c.dS)
        glev = load_blk(g_scr, gblk, t, gblk_free, "gpsimd", c.dG)
        nkc = 4 * t + 4
        ev_s_seg = [None] * nkc
        for u in range(t + 1):
            klev = load_blk(k_scr, kblk[u % 2], u, kblk_free[u % 2],
                            "sync", c.dS)
            for jj in range(4):
                j = 4 * u + jj
                ia, acc, fr = c.acc.acquire()
                em.wait_evt("tensor", fr)
                em.wait_evt("tensor", klev)
                em.wait_evt("tensor", qlev)
                for k in range(DC):
                    mme = em.op(
                        "tensor", "matmul", acc[:, :],
                        kblk[u % 2][:, k * 512 + jj * 128:k * 512 + (jj + 1) * 128],
                        qblk[t % 2][:, k * 512:(k + 1) * 512],
                        start=(k == 0), stop=(k == DC - 1),
                        inc=([(c.tM, 1)] if k == DC - 1 else None))
                em.wait_evt("vector", mme)
                em.wait_evt("vector", Ssb_free)
                if u == t:
                    em.wait_evt("vector", lm_ev)
                    me = em.op("vector", "tensor_tensor",
                               S_sb[:, j * 512:(j + 1) * 512], acc[:, :],
                               lmask[:, jj * 512:(jj + 1) * 512], ALU.mult,
                               inc=[(c.vV, 1)])
                else:
                    me = em.op("vector", "tensor_copy",
                               S_sb[:, j * 512:(j + 1) * 512], acc[:, :],
                               inc=[(c.vV, 1)])
                c.acc.release(ia, me)
                ev_s_seg[j] = me
            kblk_free[u % 2] = mme
        for dch in range(DC):
            ia, acc, fr = c.acc.acquire()
            em.wait_evt("tensor", fr)
            em.wait_evt("tensor", V_ready)
            for j in range(nkc):
                em.wait_evt("tensor", ev_s_seg[j])
                mme = em.op("tensor", "matmul", acc[:, :],
                            AR_V[:, j * D + dch * 128:j * D + (dch + 1) * 128],
                            S_sb[:, j * 512:(j + 1) * 512],
                            start=(j == 0), stop=(j == nkc - 1),
                            inc=([(c.tM, 1)] if j == nkc - 1 else None))
            em.wait_evt("vector", mme)
            em.wait_evt("vector", glev)
            em.wait_evt("vector", og_free[ogi % 2])
            ge = em.op("vector", "tensor_tensor", og_sb[ogi % 2][:, :],
                       acc[:, :], gblk[:, dch * 512:(dch + 1) * 512],
                       ALU.mult, inc=[(c.vV, 1)])
            c.acc.release(ia, ge)
            em.wait_evt("sync", ge)
            og_free[ogi % 2] = em.op(
                "sync", "dma_start",
                og_scr[t * D + dch * 128:t * D + (dch + 1) * 128, :],
                og_sb[ogi % 2][:, :], inc=[(c.dS, 16)])
            ogi += 1
        qblk_free[t % 2] = mme
        gblk_free = ge
        Ssb_free = mme
    xblk_free[0] = qblk_free[0]
    xblk_free[1] = qblk_free[1]

    a2_done = (c.dS, c.dS.n)
    nat_free3 = [V_ready, V_ready]
    em.wait_evt("gpsimd", Ssb_free)
    sr = emit_wt_load(c, Wp, AR_W, 0, nat3, nat_free3, DC, D, Ssb_free)
    ogblk = xblk[0]
    xres = xblk[1]
    Y = g.G4[:, 0:SEG]
    ogblk_free = qblk_free[0]
    xres_free = qblk_free[1]
    Y_free = gblk_free
    for nt in range(NT):
        em.wait("sync", *a2_done)
        olev = load_blk(og_scr, ogblk, nt, ogblk_free, "sync", c.dS)
        em.wait_evt("gpsimd", xres_free)
        xr_dst = xres.rearrange("p (cc n) -> p cc n", cc=DC)
        xlev = em.op("gpsimd", "dma_start", xr_dst,
                     xT_re[:, :, nt * 512:(nt + 1) * 512], inc=[(c.dG, 16)])
        stats_evt = None
        sq_evt = Ssb_free
        for o in range(DC):
            ia, acc, fr = c.acc.acquire()
            em.wait_evt("tensor", fr)
            em.wait_evt("tensor", sr[o])
            em.wait_evt("tensor", olev)
            for k in range(DC):
                mme = em.op("tensor", "matmul", acc[:, :],
                            AR_W[:, o * D + k * 128:o * D + (k + 1) * 128],
                            ogblk[:, k * 512:(k + 1) * 512],
                            start=(k == 0), stop=(k == DC - 1),
                            inc=([(c.tM, 1)] if k == DC - 1 else None))
            em.wait_evt("vector", mme)
            em.wait_evt("vector", xlev)
            em.wait_evt("vector", Y_free)
            ye = em.op("vector", "tensor_tensor", Y[:, o * 512:(o + 1) * 512],
                       acc[:, :], xres[:, o * 512:(o + 1) * 512], ALU.add,
                       inc=[(c.vV, 1)])
            c.acc.release(ia, ye)
            stats_evt, sq_evt = emit_stats_mms(
                c, Y[:, o * 512:(o + 1) * 512], ye, sq, sq_evt,
                first=(o == 0), last=(o == DC - 1))
        ogblk_free = mme
        xres_free = ye
        A_t, B_t = emit_ln_finalize(c, ln_sc, stats_evt)

        def writer(o, s0, nt=nt):
            p = o % 2
            em.wait_evt("vector", piece_free[p])
            pe = em.op("vector", "tensor_tensor", piece[p][:, :], s0[:, :],
                       B_t[1][:, :], ALU.add, inc=[(c.vV, 1)])
            em.wait_evt("sync", pe)
            de = em.op("sync", "dma_start",
                       x1_out[o * 128:(o + 1) * 128, nt * 512:(nt + 1) * 512],
                       piece[p][:, :], inc=[(c.dS, 16)])
            piece_free[p] = de
            return pe

        Y_free = emit_ln_apply(c, g, Y, writer, A_t, B_t)



def emit_block_B(c, g, dr):
    em = c.em
    x1 = dr["x1"]
    Wup, Wgt, Wdn = dr["Wup"], dr["Wgt"], dr["Wdn"]
    H2048, H128 = dr["H2048"], dr["H128"]
    SUU, SUD, WSU, WSG, WSD = dr["SUU"], dr["SUD"], dr["WSU"], dr["WSG"], dr["WSD"]
    x2r_out = dr["x2r"]
    h_scr, wdT_scr = dr["h_scr"], dr["wdT_scr"]

    AR_H = g.G1
    AR_X2 = g.G2
    AR_X = g.G3
    vecs = g.vecsB
    suu_sb = vecs[:, 0:DC]
    wsd64 = vecs[:, DC:2 * DC]
    wsu_su = g.S_sb[:, 0:IC]
    wsg64 = g.S_sb[:, IC:2 * IC]
    sud_sb = g.S_sb[:, 2 * IC:3 * IC]
    wsu_sb = g.S_sb[:, 3 * IC:4 * IC]
    wsg_sb = g.S_sb[:, 4 * IC:5 * IC]
    wsd_sb = g.S_sb[:, 5 * IC:5 * IC + DC]
    scr = g.scr
    piece = g.piece
    ln_sc = g.ln
    h128_sb = g.h128
    sq = AR_X2[:, 25600:26112]
    scr_free = g.scr_free
    piece_free = g.piece_free

    for v_sb, v_d in ((suu_sb, SUU), (sud_sb, dr["SUD"]), (wsu_sb, WSU),
                      (wsg_sb, WSG), (wsd_sb, WSD)):
        em.op("sync", "dma_start", v_sb, v_d[:, :], inc=[(c.dS, 16)])
    em.op("sync", "dma_start", h128_sb, H128[:, :], inc=[(c.dS, 16)])
    em.wait("vector", c.dS, c.dS.n)
    em.op("vector", "tensor_tensor", wsu_su, wsu_sb, sud_sb, ALU.mult)
    vv1 = em.op("vector", "tensor_scalar", wsu_su, wsu_su, 64.0, None,
                ALU.mult, inc=[(c.vV, 1)])
    vv2 = em.op("vector", "tensor_scalar", wsg64, wsg_sb, 64.0, None,
                ALU.mult, inc=[(c.vV, 1)])
    vv3 = em.op("vector", "tensor_scalar", wsd64, wsd_sb, 64.0, None,
                ALU.mult, inc=[(c.vV, 1)])
    H_re = H2048[:, :].rearrange("(c p) n -> p c n", p=128)
    hlev = em.op("sync", "dma_start",
                 AR_H[:, :].rearrange("p (c n) -> p c n", c=DC), H_re,
                 inc=[(c.dS, 16)])

    x1blk = AR_X[:, 0:SEG]
    xsblk = AR_X[:, SEG:2 * SEG]
    x1_re = x1[:, :].rearrange("(c p) n -> p c n", p=128)
    x1blk_free = None
    xs_free = None
    mme = None
    for nt in range(NT):
        em.wait_evt("gpsimd", x1blk_free)
        lev = em.op("gpsimd", "dma_start",
                    x1blk.rearrange("p (c n) -> p c n", c=DC),
                    x1_re[:, :, nt * 512:(nt + 1) * 512], inc=[(c.dG, 16)])
        sev = None
        for ch in range(DC):
            em.wait_evt("vector", lev)
            em.wait_evt("vector", xs_free)
            sev = em.op("vector", "tensor_scalar",
                        xsblk[:, ch * 512:(ch + 1) * 512],
                        x1blk[:, ch * 512:(ch + 1) * 512],
                        suu_sb[:, ch:ch + 1], None, ALU.mult,
                        inc=[(c.vV, 1)])
        x1blk_free = sev
        for o in range(DC):
            ia, acc, fr = c.acc.acquire()
            em.wait_evt("tensor", fr)
            em.wait_evt("tensor", hlev)
            em.wait_evt("tensor", sev)
            for k in range(DC):
                mme = em.op("tensor", "matmul", acc[:, :],
                            AR_H[:, k * D + o * 128:k * D + (o + 1) * 128],
                            xsblk[:, k * 512:(k + 1) * 512],
                            start=(k == 0), stop=(k == DC - 1),
                            inc=([(c.tM, 1)] if k == DC - 1 else None))
            em.wait_evt("scalar", mme)
            ce = em.op("scalar", "activation",
                       AR_X2[:, o * D + nt * 512:o * D + (nt + 1) * 512],
                       acc[:, :], AF.Copy, scale=S1, inc=[(c.sA, 1)])
            c.acc.release(ia, ce)
        xs_free = mme
    x2_ready = (c.sA, c.sA.n)
    ar_h_free = mme

    natU = [AR_X[:, i * D:(i + 1) * D] for i in range(2)]
    natG = [AR_X[:, (2 + i) * D:(3 + i) * D] for i in range(2)]
    wtU = [AR_X[:, (4 + i) * D:(5 + i) * D] for i in range(2)]
    wtG = [AR_X[:, (6 + i) * D:(7 + i) * D] for i in range(2)]
    natU_free = [x1blk_free, x1blk_free]
    natG_free = [x1blk_free, x1blk_free]
    wtU_free = [xs_free, xs_free]
    wtG_free = [xs_free, xs_free]

    def slab_load(W_dram, cc, natb, nat_free_l, slot, wt_dst, wt_free_l):
        em.wait_evt("gpsimd", nat_free_l[slot])
        ld = em.op("gpsimd", "dma_start", natb[slot],
                   W_dram[cc * 128:(cc + 1) * 128, :], inc=[(c.dG, 16)])
        tev = None
        evac = None
        for gg in range(4):
            bi, bt, bfree = c.ptr.acquire()
            em.wait_evt("tensor", ld)
            em.wait_evt("tensor", bfree)
            for i in range(4):
                k = gg * 4 + i
                inc = [(c.tM, 1)] if i == 3 else None
                tev = em.op("tensor", "transpose",
                            bt[:, i * 128:(i + 1) * 128],
                            natb[slot][:, k * 128:(k + 1) * 128],
                            c.idn[:, :], inc=inc)
            em.wait_evt("vector", tev)
            em.wait_evt("vector", wt_free_l[slot])
            evac = em.op("vector", "tensor_copy",
                         wt_dst[slot][:, gg * 512:(gg + 1) * 512],
                         bt[:, 0:512], inc=[(c.vV, 1)])
            c.ptr.release(bi, evac)
        nat_free_l[slot] = tev
        return evac

    hp_free = [None, None]
    mmu = mmg = None
    for cc in range(IC):
        sl = cc % 2
        uev = slab_load(Wup, cc, natU, natU_free, sl, wtU, wtU_free)
        gev = slab_load(Wgt, cc, natG, natG_free, sl, wtG, wtG_free)
        for nt in range(NT):
            iu, pu, fu = c.acc.acquire()
            em.wait_evt("tensor", fu)
            em.wait_evt("tensor", uev)
            em.wait_evt("tensor", x2_ready)
            for k in range(DC):
                mmu = em.op("tensor", "matmul", pu[:, :],
                            wtU[sl][:, k * 128:(k + 1) * 128],
                            AR_X2[:, k * D + nt * 512:k * D + (nt + 1) * 512],
                            start=(k == 0), stop=(k == DC - 1),
                            inc=([(c.tM, 1)] if k == DC - 1 else None))
            ig, pg, fg = c.acc.acquire()
            em.wait_evt("tensor", fg)
            em.wait_evt("tensor", gev)
            for k in range(DC):
                mmg = em.op("tensor", "matmul", pg[:, :],
                            wtG[sl][:, k * 128:(k + 1) * 128],
                            AR_X2[:, k * D + nt * 512:k * D + (nt + 1) * 512],
                            start=(k == 0), stop=(k == DC - 1),
                            inc=([(c.tM, 1)] if k == DC - 1 else None))
            em.wait_evt("scalar", mmg)
            em.wait_evt("scalar", scr_free[0])
            em.wait_evt("scalar", vv2)
            sg = em.op("scalar", "activation", scr[0][:, :], pg[:, :],
                       AF.Silu, scale=wsg64[:, cc:cc + 1], inc=[(c.sA, 1)])
            c.acc.release(ig, sg)
            em.wait_evt("vector", mmu)
            em.wait_evt("vector", vv1)
            em.wait_evt("vector", scr_free[1])
            us = em.op("vector", "tensor_scalar", scr[1][:, :], pu[:, :],
                       wsu_su[:, cc:cc + 1], None, ALU.mult, inc=[(c.vV, 1)])
            c.acc.release(iu, us)
            p = nt % 2
            em.wait_evt("vector", sg)
            em.wait_evt("vector", hp_free[p])
            he = em.op("vector", "tensor_tensor", piece[p][:, :],
                       scr[0][:, :], scr[1][:, :], ALU.mult, inc=[(c.vV, 1)])
            scr_free[0] = he
            scr_free[1] = he
            em.wait_evt("sync", he)
            hp_free[p] = em.op(
                "sync", "dma_start",
                h_scr[nt * INTER + cc * 128:nt * INTER + (cc + 1) * 128, :],
                piece[p][:, :], inc=[(c.dS, 16)])
        wtU_free[sl] = mmu
        wtG_free[sl] = mmg

    nat8 = [AR_X[:, 0:INTER], AR_X[:, INTER:2 * INTER]]
    nat8_free = [natU_free[0], natU_free[1]]
    wdp_free = [hp_free[0], hp_free[1]]
    b2_done = mmg
    tev = None
    for s in range(DC):
        sl = s % 2
        em.wait_evt("gpsimd", nat8_free[sl])
        em.wait("gpsimd", c.tM, c.tM.n)
        ld = em.op("gpsimd", "dma_start", nat8[sl],
                   Wdn[s * 128:(s + 1) * 128, :], inc=[(c.dG, 16)])
        for gg in range(16):
            bi, bt, bfree = c.ptr.acquire()
            em.wait_evt("tensor", ld)
            em.wait_evt("tensor", bfree)
            for i in range(4):
                k = gg * 4 + i
                inc = [(c.tM, 1)] if i == 3 else None
                tev = em.op("tensor", "transpose",
                            bt[:, i * 128:(i + 1) * 128],
                            nat8[sl][:, k * 128:(k + 1) * 128],
                            c.idn[:, :], inc=inc)
            p = gg % 2
            em.wait_evt("vector", tev)
            em.wait_evt("vector", wdp_free[p])
            ev = em.op("vector", "tensor_copy", piece[p][:, :], bt[:, 0:512],
                       inc=[(c.vV, 1)])
            c.ptr.release(bi, ev)
            em.wait_evt("sync", ev)
            dst = wdT_scr[s * INTER + gg * 512:s * INTER + (gg + 1) * 512, :]
            dst3 = dst.rearrange("(cc i) o -> i cc o", i=128)
            src3 = piece[p][:, :].rearrange("i (cc o) -> i cc o", cc=4)
            wdp_free[p] = em.op("sync", "dma_start", dst3, src3,
                                inc=[(c.dS, 16)])
        nat8_free[sl] = tev
    piece_free[0] = wdp_free[0]
    piece_free[1] = wdp_free[1]
    wd_prep_done = (c.dS, c.dS.n)

    hsup = AR_H
    wdT = [AR_X[:, i * INTER:(i + 1) * INTER] for i in range(2)]
    Y = AR_X2[:, 0:SEG]
    x1p = [AR_X2[:, SEG + i * 512:SEG + (i + 1) * 512] for i in range(2)]
    hsup_free = ar_h_free
    wdT_free = [nat8_free[0], nat8_free[1]]
    Y_free = b2_done
    x1p_free = [b2_done, b2_done]
    sq_evt0 = b2_done
    for sup in range(NT):
        em.wait("sync", *wd_prep_done)
        em.wait_evt("sync", hsup_free)
        hld = em.op("sync", "dma_start",
                    hsup[:, :].rearrange("p (cc n) -> p cc n", cc=IC),
                    h_scr[sup * INTER:(sup + 1) * INTER, :]
                    .rearrange("(cc p) n -> p cc n", p=128),
                    inc=[(c.dS, 16)])
        h2ev = None
        for cc in range(IC):
            ia, acc, fr = c.acc.acquire()
            em.wait_evt("tensor", fr)
            em.wait_evt("tensor", hld)
            mme = em.op("tensor", "matmul", acc[:, :], h128_sb,
                        hsup[:, cc * 512:(cc + 1) * 512],
                        start=True, stop=True, inc=[(c.tM, 1)])
            em.wait_evt("scalar", mme)
            h2ev = em.op("scalar", "activation",
                         hsup[:, cc * 512:(cc + 1) * 512], acc[:, :],
                         AF.Copy, scale=S2, inc=[(c.sA, 1)])
            c.acc.release(ia, h2ev)
        em.wait_evt("vector", h2ev)
        em.wait("vector", c.sA, c.sA.n)
        bev = None
        hh = 1
        while hh < IC:
            for blk in range(0, IC, 2 * hh):
                for j2 in range(hh):
                    a_ap = hsup[:, (blk + j2) * 512:(blk + j2 + 1) * 512]
                    b_ap = hsup[:, (blk + hh + j2) * 512:(blk + hh + j2 + 1) * 512]
                    em.op("vector", "tensor_sub", b_ap, a_ap, b_ap)
                    bev = em.op("vector", "scalar_tensor_tensor", a_ap, a_ap,
                                2.0, b_ap, ALU.mult, ALU.subtract,
                                inc=[(c.vV, 1)])
            hh *= 2
        stats_evt = None
        sq_evt = sq_evt0
        for s in range(DC):
            sl = s % 2
            em.wait_evt("sync", wdT_free[sl])
            wld = em.op("sync", "dma_start",
                        wdT[sl].rearrange("i (cc o) -> i cc o", cc=IC),
                        wdT_scr[s * INTER:(s + 1) * INTER, :]
                        .rearrange("(cc i) o -> i cc o", i=128),
                        inc=[(c.dS, 16)])
            em.wait_evt("gpsimd", x1p_free[sl])
            x1ld = em.op("gpsimd", "dma_start", x1p[sl],
                         x1[s * 128:(s + 1) * 128,
                            sup * 512:(sup + 1) * 512], inc=[(c.dG, 16)])
            ia, acc, fr = c.acc.acquire()
            em.wait_evt("tensor", fr)
            em.wait_evt("tensor", wld)
            em.wait_evt("tensor", bev)
            for cci in range(IC):
                mme = em.op("tensor", "matmul", acc[:, :],
                            wdT[sl][:, cci * 128:(cci + 1) * 128],
                            hsup[:, cci * 512:(cci + 1) * 512],
                            start=(cci == 0), stop=(cci == IC - 1),
                            inc=([(c.tM, 1)] if cci == IC - 1 else None))
            wdT_free[sl] = mme
            em.wait_evt("vector", mme)
            em.wait_evt("vector", x1ld)
            em.wait_evt("vector", Y_free)
            em.wait_evt("vector", vv3)
            ye = em.op("vector", "scalar_tensor_tensor",
                       Y[:, s * 512:(s + 1) * 512], acc[:, :],
                       wsd64[:, s:s + 1], x1p[sl], ALU.mult, ALU.add,
                       inc=[(c.vV, 1)])
            c.acc.release(ia, ye)
            x1p_free[sl] = ye
            stats_evt, sq_evt = emit_stats_mms(
                c, Y[:, s * 512:(s + 1) * 512], ye, sq, sq_evt,
                first=(s == 0), last=(s == DC - 1))
        hsup_free = mme
        A_t, B_t = emit_ln_finalize(c, ln_sc, stats_evt)

        def writer(s, s0, sup=sup):
            p = s % 2
            em.wait_evt("vector", g.piece_free[p])
            pe = em.op("vector", "tensor_tensor", g.piece[p][:, :], s0[:, :],
                       B_t[1][:, :], ALU.add, inc=[(c.vV, 1)])
            em.wait_evt("sync", pe)
            de = em.op("sync", "dma_start",
                       x2r_out[s * 128:(s + 1) * 128,
                               sup * 512:(sup + 1) * 512],
                       g.piece[p][:, :], inc=[(c.dS, 16)])
            g.piece_free[p] = de
            return pe

        Y_free = emit_ln_apply(c, g, Y, writer, A_t, B_t)



def emit_block_C(c, g, dr):
    em = c.em
    x2r = dr["x2r"]
    Wq, Wk, Wv, Wo = dr["Wq1"], dr["Wk1"], dr["Wv1"], dr["Wo1"]
    AMASK = dr["AMASK"]
    x3_out = dr["x3"]
    q_scr, k_scr, o_scr = dr["q1_scr"], dr["k1_scr"], dr["o1_scr"]

    AR_W = g.G1
    AR_V = g.G2
    xblk = [g.G3[:, 0:SEG], g.G3[:, SEG:2 * SEG]]
    nat1 = [g.G4[:, 0:2048]]
    nat3 = [g.G2[:, 512 + i * 2048:512 + (i + 1) * 2048] for i in range(2)]
    S_sb = g.S_sb
    P_sb = g.G4[:, 0:T]
    amask = g.amask
    nmax = g.rowb[:, 0:1]
    rsum = g.rowb[:, 8:9]
    rinv = g.rowb[:, 16:17]
    scr = g.scr
    piece = g.piece
    ln_sc = g.ln
    sq = AR_V[:, 0:512]
    scr_free = g.scr_free
    piece_free = g.piece_free

    xblk_free = [None, None]
    nat_free1 = [None]
    amask_ev = em.op("sync", "dma_start", amask, AMASK[:, :],
                     inc=[(c.dS, 16)])
    x_re = x2r[:, :].rearrange("(c p) n -> p c n", p=128)

    def load_xblk(nt, slot):
        em.wait_evt("gpsimd", xblk_free[slot])
        src = x_re[:, :, nt * 512:(nt + 1) * 512]
        dst = xblk[slot].rearrange("p (c n) -> p c n", c=DC)
        return em.op("gpsimd", "dma_start", dst, src, inc=[(c.dG, 16)])

    em.wait_evt("tensor", c.const_evt)

    def proj_plain(wt_view, slab_ready, spill):
        pi = [0]
        mme = None
        for nt in range(NT):
            lev = load_xblk(nt, nt % 2)
            for o in range(DC):
                ia, acc, fr = c.acc.acquire()
                em.wait_evt("tensor", fr)
                em.wait_evt("tensor", slab_ready[o])
                em.wait_evt("tensor", lev)
                for k in range(DC):
                    mme = em.op(
                        "tensor", "matmul", acc[:, :],
                        wt_view[:, o * D + k * 128:o * D + (k + 1) * 128],
                        xblk[nt % 2][:, k * 512:(k + 1) * 512],
                        start=(k == 0), stop=(k == DC - 1),
                        inc=([(c.tM, 1)] if k == DC - 1 else None))
                p = pi[0] % 2
                em.wait_evt("vector", mme)
                em.wait_evt("vector", piece_free[p])
                pe = em.op("vector", "tensor_copy", piece[p][:, :], acc[:, :],
                           inc=[(c.vV, 1)])
                c.acc.release(ia, pe)
                em.wait_evt("sync", pe)
                de = em.op("sync", "dma_start",
                           spill[nt * D + o * 128:nt * D + (o + 1) * 128, :],
                           piece[p][:, :], inc=[(c.dS, 16)])
                piece_free[p] = de
                pi[0] += 1
            xblk_free[nt % 2] = mme
        return mme

    wt_free = None
    sr = emit_wt_load(c, Wq, AR_W, 0, nat1, nat_free1, DC, D, wt_free)
    wt_free = proj_plain(AR_W, sr, q_scr)
    sr = emit_wt_load(c, Wk, AR_W, 0, nat1, nat_free1, DC, D, wt_free)
    wt_free = proj_plain(AR_W, sr, k_scr)
    sr = emit_wt_load(c, Wv, AR_W, 0, nat1, nat_free1, DC, D, wt_free)

    wtv_re = AR_W[:, :].rearrange("p (s x) -> p s x", s=DC)
    mme = None
    for m in range(4):
        for nb in range(NT):
            lev = load_xblk(nb, nb % 2)
            for ns in range(4):
                nch = nb * 4 + ns
                ia, acc, fr = c.acc.acquire()
                em.wait_evt("tensor", fr)
                em.wait_evt("tensor", sr[4 * m + 3])
                em.wait_evt("tensor", lev)
                for k in range(DC):
                    rhs = wtv_re[:, 4 * m:4 * m + 4, k * 128:(k + 1) * 128]
                    mme = em.op(
                        "tensor", "matmul", acc[:, :],
                        xblk[nb % 2][:, k * 512 + ns * 128:k * 512 + (ns + 1) * 128],
                        rhs, start=(k == 0), stop=(k == DC - 1),
                        inc=([(c.tM, 1)] if k == DC - 1 else None))
                em.wait_evt("scalar", mme)
                ve = em.op("scalar", "activation",
                           AR_V[:, nch * D + m * 512:nch * D + (m + 1) * 512],
                           acc[:, :], AF.Copy, inc=[(c.sA, 1)])
                c.acc.release(ia, ve)
            xblk_free[nb % 2] = mme
    V_ready = (c.sA, c.sA.n)
    v_mms_done = (c.tM, c.tM.n)
    c1_spill_done = (c.dS, c.dS.n)

    kblk = [AR_W[:, u * SEG:(u + 1) * SEG] for u in range(4)]
    qblk = xblk[0]
    PT_sb = g.G3[:, SEG:2 * SEG]
    PT3 = PT_sb.rearrange("p (j q) -> p j q", j=DC)

    kblk_ld = [None] * 4
    qblk_free = xblk_free[0]
    pt_free = xblk_free[1]
    S_free = None
    P_free = v_mms_done
    op_free = [piece_free[0], piece_free[1]]
    opi = 0
    for t in range(4):
        em.wait("sync", *c1_spill_done)
        em.wait_evt("sync", v_mms_done)
        em.wait_evt("sync", qblk_free)
        qlev = em.op("sync", "dma_start",
                     qblk.rearrange("p (c n) -> p c n", c=DC),
                     q_scr[t * D:(t + 1) * D, :]
                     .rearrange("(c p) n -> p c n", p=128),
                     inc=[(c.dS, 16)])
        kblk_ld[t] = em.op("sync", "dma_start",
                           kblk[t].rearrange("p (c n) -> p c n", c=DC),
                           k_scr[t * D:(t + 1) * D, :]
                           .rearrange("(c p) n -> p c n", p=128),
                           inc=[(c.dS, 16)])
        for h in range(16):
            ptw = None
            for m2 in range(4):
                lhsq = qblk[:, h * 512 + m2 * 128:h * 512 + (m2 + 1) * 128]
                for u in range(t + 1):
                    ia, acc, fr = c.acc.acquire()
                    em.wait_evt("tensor", fr)
                    em.wait_evt("tensor", qlev)
                    em.wait_evt("tensor", kblk_ld[u])
                    wdt = 512 if u < t else (m2 + 1) * 128
                    mme = em.op("tensor", "matmul", acc[:, 0:wdt], lhsq,
                                kblk[u][:, h * 512:h * 512 + wdt],
                                start=True, stop=True, inc=[(c.tM, 1)])
                    em.wait_evt("vector", mme)
                    em.wait_evt("vector", S_free)
                    if u < t:
                        se = em.op("vector", "tensor_scalar",
                                   S_sb[:, u * 512:(u + 1) * 512],
                                   acc[:, 0:512], ISQ, None, ALU.mult,
                                   inc=[(c.vV, 1)])
                    else:
                        if m2 > 0:
                            em.op("vector", "tensor_scalar",
                                  S_sb[:, t * 512:t * 512 + m2 * 128],
                                  acc[:, 0:m2 * 128], ISQ, None, ALU.mult)
                        em.wait_evt("vector", amask_ev)
                        se = em.op("vector", "scalar_tensor_tensor",
                                   S_sb[:, t * 512 + m2 * 128:t * 512 + (m2 + 1) * 128],
                                   acc[:, m2 * 128:(m2 + 1) * 128], ISQ,
                                   amask, ALU.mult, ALU.add,
                                   inc=[(c.vV, 1)])
                    c.acc.release(ia, se)
                W = t * 512 + (m2 + 1) * 128
                nw = em.op("vector", "reduce_max", nmax, S_sb[:, 0:W],
                           axis=AX.X, negate=True, inc=[(c.vV, 1)])
                em.wait_evt("scalar", nw)
                em.wait_evt("scalar", P_free)
                ex = em.op("scalar", "activation", P_sb[:, 0:W],
                           S_sb[:, 0:W], AF.Exp, bias=nmax, inc=[(c.sA, 1)])
                S_free = ex
                em.wait_evt("vector", ex)
                rse = em.op("vector", "reduce_sum", rsum, P_sb[:, 0:W],
                            axis=AX.X, inc=[(c.vV, 1)])
                em.wait_evt("vector", rse)
                rie = em.op("vector", "reciprocal", rinv, rsum,
                            inc=[(c.vV, 1)])
                em.wait_evt("vector", rie)
                pn = em.op("vector", "tensor_scalar", P_sb[:, 0:W],
                           P_sb[:, 0:W], rinv, None, ALU.mult,
                           inc=[(c.vV, 1)])
                ntr = 4 * t + m2 + 1
                tev = None
                for gg in range((ntr + 3) // 4):
                    gn = min(4, ntr - gg * 4)
                    bi, bt, bfree = c.ptr.acquire()
                    em.wait_evt("tensor", pn)
                    em.wait_evt("tensor", bfree)
                    for i in range(gn):
                        j2 = gg * 4 + i
                        inc = [(c.tM, 1)] if i == gn - 1 else None
                        tev = em.op("tensor", "transpose",
                                    bt[:, i * 128:(i + 1) * 128],
                                    P_sb[:, j2 * 128:(j2 + 1) * 128],
                                    c.idn[:, :], inc=inc)
                    em.wait_evt("vector", tev)
                    em.wait_evt("vector", pt_free)
                    ptv = em.op("vector", "tensor_copy",
                                PT3[:, gg * 4:gg * 4 + gn,
                                    m2 * 128:(m2 + 1) * 128],
                                bt[:, 0:gn * 128].rearrange(
                                    "p (j q) -> p j q", j=gn),
                                inc=[(c.vV, 1)])
                    c.ptr.release(bi, ptv)
                P_free = tev
                ptv2 = ptv
                for j2 in range(ntr, 4 * t + 4):
                    em.wait_evt("vector", pt_free)
                    ptv2 = em.op("vector", "memset",
                                 PT3[:, j2:j2 + 1, m2 * 128:(m2 + 1) * 128],
                                 0.0, inc=[(c.vV, 1)])
                ptw = ptv2
            ia, acc, fr = c.acc.acquire()
            em.wait_evt("tensor", fr)
            em.wait_evt("tensor", V_ready)
            em.wait_evt("tensor", ptw)
            for j2 in range(4 * t + 4):
                mme = em.op("tensor", "matmul", acc[:, :],
                            AR_V[:, j2 * D + h * 128:j2 * D + (h + 1) * 128],
                            PT_sb[:, j2 * 512:(j2 + 1) * 512],
                            start=(j2 == 0), stop=(j2 == 4 * t + 3),
                            inc=([(c.tM, 1)] if j2 == 4 * t + 3 else None))
            pt_free = mme
            p = opi % 2
            em.wait_evt("vector", mme)
            em.wait_evt("vector", op_free[p])
            oe = em.op("vector", "tensor_copy", piece[p][:, :], acc[:, :],
                       inc=[(c.vV, 1)])
            c.acc.release(ia, oe)
            em.wait_evt("sync", oe)
            op_free[p] = em.op("sync", "dma_start",
                               o_scr[t * D + h * 128:t * D + (h + 1) * 128, :],
                               piece[p][:, :], inc=[(c.dS, 16)])
            opi += 1
        qblk_free = mme

    piece_free[0] = op_free[0]
    piece_free[1] = op_free[1]
    c2_done = (c.dS, c.dS.n)

    nat_free3 = [pt_free, pt_free]
    sr = emit_wt_load(c, Wo, AR_W, 0, nat3, nat_free3, DC, D, mme)
    ogblk = xblk[0]
    xres = xblk[1]
    Y = g.G4[:, 0:SEG]
    ogblk_free = qblk_free
    xres_free = pt_free
    Y_free = P_free
    for nt in range(NT):
        em.wait("sync", *c2_done)
        em.wait_evt("sync", ogblk_free)
        olev = em.op("sync", "dma_start",
                     ogblk.rearrange("p (c n) -> p c n", c=DC),
                     o_scr[nt * D:(nt + 1) * D, :]
                     .rearrange("(c p) n -> p c n", p=128),
                     inc=[(c.dS, 16)])
        em.wait_evt("gpsimd", xres_free)
        xlev = em.op("gpsimd", "dma_start",
                     xres.rearrange("p (cc n) -> p cc n", cc=DC),
                     x_re[:, :, nt * 512:(nt + 1) * 512], inc=[(c.dG, 16)])
        stats_evt = None
        sq_evt = pt_free
        for o in range(DC):
            ia, acc, fr = c.acc.acquire()
            em.wait_evt("tensor", fr)
            em.wait_evt("tensor", sr[o])
            em.wait_evt("tensor", olev)
            for k in range(DC):
                mme = em.op("tensor", "matmul", acc[:, :],
                            AR_W[:, o * D + k * 128:o * D + (k + 1) * 128],
                            ogblk[:, k * 512:(k + 1) * 512],
                            start=(k == 0), stop=(k == DC - 1),
                            inc=([(c.tM, 1)] if k == DC - 1 else None))
            em.wait_evt("vector", mme)
            em.wait_evt("vector", xlev)
            em.wait_evt("vector", Y_free)
            ye = em.op("vector", "tensor_tensor", Y[:, o * 512:(o + 1) * 512],
                       acc[:, :], xres[:, o * 512:(o + 1) * 512], ALU.add,
                       inc=[(c.vV, 1)])
            c.acc.release(ia, ye)
            stats_evt, sq_evt = emit_stats_mms(
                c, Y[:, o * 512:(o + 1) * 512], ye, sq, sq_evt,
                first=(o == 0), last=(o == DC - 1))
        ogblk_free = mme
        xres_free = ye
        A_t, B_t = emit_ln_finalize(c, ln_sc, stats_evt)

        def writer(o, s0, nt=nt):
            p = o % 2
            em.wait_evt("vector", piece_free[p])
            pe = em.op("vector", "tensor_tensor", piece[p][:, :], s0[:, :],
                       B_t[1][:, :], ALU.add, inc=[(c.vV, 1)])
            em.wait_evt("sync", pe)
            de = em.op("sync", "dma_start",
                       x3_out[o * 128:(o + 1) * 128, nt * 512:(nt + 1) * 512],
                       piece[p][:, :], inc=[(c.dS, 16)])
            piece_free[p] = de
            return pe

        Y_free = emit_ln_apply(c, g, Y, writer, A_t, B_t)



def emit_block_D(c, g, dr):
    em = c.em
    x3 = dr["x3"]
    Wup, Wdn = dr["Wup1"], dr["Wdn1"]
    UPS, DNS, OSC = dr["UPS"], dr["DNS"], dr["OSC"]
    yT_out = dr["yT"]
    hid_scr, wdT_scr = dr["hid_scr"], dr["wd1T_scr"]

    AR_A = g.G1
    AR_B2 = g.G3
    upsc = g.vecsB[:, 0:IC]
    dns_sb = g.vecsD[:, 0:DC]
    dsos = g.vecsD[:, DC:2 * DC]
    osc_sb = g.vecsD[:, 2 * DC:2 * DC + 1]
    scr = g.scr
    piece = g.piece
    fpiece = [g.FA[:, 2560 + i * 512:2560 + (i + 1) * 512] for i in range(2)]
    sq = g.G2[:, 0:512]
    ln_sc = g.ln
    scr_free = g.scr_free
    piece_free = g.piece_free
    fpiece_free = [None, None]

    for v_sb, v_d in ((upsc, UPS), (dns_sb, DNS), (osc_sb, OSC)):
        em.op("sync", "dma_start", v_sb, v_d[:, :], inc=[(c.dS, 16)])
    em.wait("vector", c.dS, c.dS.n)
    vv1 = em.op("vector", "tensor_scalar", dsos, dns_sb, osc_sb, None,
                ALU.mult, inc=[(c.vV, 1)])

    x3_re = x3[:, :].rearrange("(c p) n -> p c n", p=128)
    x3ld = em.op("gpsimd", "dma_start",
                 AR_A[:, :].rearrange("p (c n) -> p c n", c=DC),
                 x3_re[:, :, :], inc=[(c.dG, 16)])

    natU = [AR_B2[:, i * D:(i + 1) * D] for i in range(2)]
    natS = AR_B2[:, 2 * D:3 * D]
    wtU = [AR_B2[:, (3 + i) * D:(4 + i) * D] for i in range(2)]
    natU_free = [None, None]
    natS_free = [None]
    wtU_free = [None, None]
    hp_free = [None, None]

    em.wait_evt("tensor", c.const_evt)
    mme = None
    for cc in range(IC):
        sl = cc % 2
        em.wait_evt("gpsimd", natU_free[sl])
        ld = em.op("gpsimd", "dma_start", natU[sl],
                   Wup[cc * 128:(cc + 1) * 128, :], inc=[(c.dG, 16)])
        em.wait_evt("scalar", ld)
        em.wait_evt("scalar", natS_free[0])
        sg = em.op("scalar", "activation", natS, natU[sl], AF.Sign,
                   inc=[(c.sA, 1)])
        natU_free[sl] = sg
        tev = None
        evac = None
        for gg in range(4):
            bi, bt, bfree = c.ptr.acquire()
            em.wait_evt("tensor", sg)
            em.wait_evt("tensor", bfree)
            for i in range(4):
                k = gg * 4 + i
                inc = [(c.tM, 1)] if i == 3 else None
                tev = em.op("tensor", "transpose",
                            bt[:, i * 128:(i + 1) * 128],
                            natS[:, k * 128:(k + 1) * 128],
                            c.idn[:, :], inc=inc)
            em.wait_evt("vector", tev)
            em.wait_evt("vector", wtU_free[sl])
            evac = em.op("vector", "tensor_copy",
                         wtU[sl][:, gg * 512:(gg + 1) * 512],
                         bt[:, 0:512], inc=[(c.vV, 1)])
            c.ptr.release(bi, evac)
        natS_free[0] = tev
        for nt in range(NT):
            ia, acc, fr = c.acc.acquire()
            em.wait_evt("tensor", fr)
            em.wait_evt("tensor", evac)
            em.wait_evt("tensor", x3ld)
            for k in range(DC):
                mme = em.op("tensor", "matmul", acc[:, :],
                            wtU[sl][:, k * 128:(k + 1) * 128],
                            AR_A[:, k * D + nt * 512:k * D + (nt + 1) * 512],
                            start=(k == 0), stop=(k == DC - 1),
                            inc=([(c.tM, 1)] if k == DC - 1 else None))
            p = nt % 2
            em.wait_evt("scalar", mme)
            em.wait_evt("scalar", hp_free[p])
            he = em.op("scalar", "activation", piece[p][:, :], acc[:, :],
                       AF.Relu, scale=upsc[:, cc:cc + 1], inc=[(c.sA, 1)])
            c.acc.release(ia, he)
            em.wait_evt("sync", he)
            hp_free[p] = em.op(
                "sync", "dma_start",
                hid_scr[nt * INTER + cc * 128:nt * INTER + (cc + 1) * 128, :],
                piece[p][:, :], inc=[(c.dS, 16)])
        wtU_free[sl] = mme
    d1_mms_done = (c.tM, c.tM.n)

    nat8 = AR_B2[:, 0:INTER]
    natS8 = AR_B2[:, INTER:2 * INTER]
    nat8_free = d1_mms_done
    natS8_free = d1_mms_done
    wdp_free = [hp_free[0], hp_free[1]]
    tev = None
    for s in range(DC):
        em.wait_evt("gpsimd", nat8_free)
        em.wait("gpsimd", c.tM, c.tM.n)
        ld = em.op("gpsimd", "dma_start", nat8,
                   Wdn[s * 128:(s + 1) * 128, :], inc=[(c.dG, 16)])
        em.wait_evt("scalar", ld)
        em.wait_evt("scalar", natS8_free)
        sg = em.op("scalar", "activation", natS8, nat8, AF.Sign,
                   inc=[(c.sA, 1)])
        nat8_free = sg
        for gg in range(16):
            bi, bt, bfree = c.ptr.acquire()
            em.wait_evt("tensor", sg)
            em.wait_evt("tensor", bfree)
            for i in range(4):
                k = gg * 4 + i
                inc = [(c.tM, 1)] if i == 3 else None
                tev = em.op("tensor", "transpose",
                            bt[:, i * 128:(i + 1) * 128],
                            natS8[:, k * 128:(k + 1) * 128],
                            c.idn[:, :], inc=inc)
            p = gg % 2
            em.wait_evt("vector", tev)
            em.wait_evt("vector", wdp_free[p])
            ev = em.op("vector", "tensor_copy", piece[p][:, :], bt[:, 0:512],
                       inc=[(c.vV, 1)])
            c.ptr.release(bi, ev)
            em.wait_evt("sync", ev)
            dst = wdT_scr[s * INTER + gg * 512:s * INTER + (gg + 1) * 512, :]
            dst3 = dst.rearrange("(cc i) o -> i cc o", i=128)
            src3 = piece[p][:, :].rearrange("i (cc o) -> i cc o", cc=4)
            wdp_free[p] = em.op("sync", "dma_start", dst3, src3,
                                inc=[(c.dS, 16)])
        natS8_free = tev
    piece_free[0] = wdp_free[0]
    piece_free[1] = wdp_free[1]
    wd_prep_done = (c.dS, c.dS.n)

    hsup = AR_A
    wdT = [AR_B2[:, i * INTER:(i + 1) * INTER] for i in range(2)]
    Y = g.G4[:, 0:SEG]
    x3p = g.x3p
    hsup_free = d1_mms_done
    wdT_free = [natS8_free, natS8_free]
    Y_free = None
    x3p_free = [None, None]
    for sup in range(NT):
        em.wait("sync", *wd_prep_done)
        em.wait_evt("sync", hsup_free)
        hld = em.op("sync", "dma_start",
                    hsup[:, :].rearrange("p (cc n) -> p cc n", cc=IC),
                    hid_scr[sup * INTER:(sup + 1) * INTER, :]
                    .rearrange("(cc p) n -> p cc n", p=128),
                    inc=[(c.dS, 16)])
        stats_evt = None
        sq_evt = None
        for s in range(DC):
            sl = s % 2
            em.wait_evt("sync", wdT_free[sl])
            wld = em.op("sync", "dma_start",
                        wdT[sl].rearrange("i (cc o) -> i cc o", cc=IC),
                        wdT_scr[s * INTER:(s + 1) * INTER, :]
                        .rearrange("(cc i) o -> i cc o", i=128),
                        inc=[(c.dS, 16)])
            em.wait_evt("gpsimd", x3p_free[sl])
            x3ld2 = em.op("gpsimd", "dma_start", x3p[sl],
                          x3[s * 128:(s + 1) * 128,
                             sup * 512:(sup + 1) * 512], inc=[(c.dG, 16)])
            ia, acc, fr = c.acc.acquire()
            em.wait_evt("tensor", fr)
            em.wait_evt("tensor", wld)
            em.wait_evt("tensor", hld)
            for cci in range(IC):
                mme = em.op("tensor", "matmul", acc[:, :],
                            wdT[sl][:, cci * 128:(cci + 1) * 128],
                            hsup[:, cci * 512:(cci + 1) * 512],
                            start=(cci == 0), stop=(cci == IC - 1),
                            inc=([(c.tM, 1)] if cci == IC - 1 else None))
            wdT_free[sl] = mme
            em.wait_evt("vector", mme)
            em.wait_evt("vector", x3ld2)
            em.wait_evt("vector", Y_free)
            em.wait_evt("vector", vv1)
            ye = em.op("vector", "scalar_tensor_tensor",
                       Y[:, s * 512:(s + 1) * 512], acc[:, :],
                       dsos[:, s:s + 1], x3p[sl], ALU.mult, ALU.add,
                       inc=[(c.vV, 1)])
            c.acc.release(ia, ye)
            x3p_free[sl] = ye
            stats_evt, sq_evt = emit_stats_mms(
                c, Y[:, s * 512:(s + 1) * 512], ye, sq, sq_evt,
                first=(s == 0), last=(s == DC - 1))
        hsup_free = mme
        A_t, B_t = emit_ln_finalize(c, ln_sc, stats_evt)

        def writer(s, s0, sup=sup):
            p = s % 2
            em.wait_evt("vector", fpiece_free[p])
            pe = em.op("vector", "tensor_tensor", fpiece[p][:, :], s0[:, :],
                       B_t[1][:, :], ALU.add, inc=[(c.vV, 1)])
            em.wait_evt("sync", pe)
            de = em.op("sync", "dma_start",
                       yT_out[s * 128:(s + 1) * 128,
                              sup * 512:(sup + 1) * 512],
                       fpiece[p][:, :], inc=[(c.dS, 16)])
            fpiece_free[p] = de
            return pe

        Y_free = emit_ln_apply(c, g, Y, writer, A_t, B_t)



def build_full():
    nc = bass.Bass("TRN2", target_bir_lowering=False, debug=False)
    stack = contextlib.ExitStack()
    dr = {}
    dr["xT"] = nc.dram_tensor("xT", [D, T], F32, kind="ExternalInput")
    for nm in ("Wq", "Wk", "Wv", "Wg", "Wp"):
        dr[nm] = nc.dram_tensor(nm, [D, D], F32, kind="ExternalInput")
    dr["LMASK"] = nc.dram_tensor("LMASK", [4 * 128, 512], BF16,
                                 kind="ExternalInput")
    dr["Wup"] = nc.dram_tensor("Wup", [INTER, D], F32, kind="ExternalInput")
    dr["Wgt"] = nc.dram_tensor("Wgt", [INTER, D], F32, kind="ExternalInput")
    dr["Wdn"] = nc.dram_tensor("Wdn", [D, INTER], F32, kind="ExternalInput")
    dr["H2048"] = nc.dram_tensor("H2048", [D, D], BF16, kind="ExternalInput")
    dr["H128"] = nc.dram_tensor("H128", [128, 128], BF16, kind="ExternalInput")
    dr["SUU"] = nc.dram_tensor("SUU", [128, DC], F32, kind="ExternalInput")
    dr["SUD"] = nc.dram_tensor("SUD", [128, IC], F32, kind="ExternalInput")
    dr["WSU"] = nc.dram_tensor("WSU", [128, IC], F32, kind="ExternalInput")
    dr["WSG"] = nc.dram_tensor("WSG", [128, IC], F32, kind="ExternalInput")
    dr["WSD"] = nc.dram_tensor("WSD", [128, DC], F32, kind="ExternalInput")
    for nm in ("Wq1", "Wk1", "Wv1", "Wo1"):
        dr[nm] = nc.dram_tensor(nm, [D, D], F32, kind="ExternalInput")
    dr["AMASK"] = nc.dram_tensor("AMASK", [128, 128], BF16,
                                 kind="ExternalInput")
    dr["Wup1"] = nc.dram_tensor("Wup1", [INTER, D], F32, kind="ExternalInput")
    dr["Wdn1"] = nc.dram_tensor("Wdn1", [D, INTER], F32, kind="ExternalInput")
    dr["UPS"] = nc.dram_tensor("UPS", [128, IC], F32, kind="ExternalInput")
    dr["DNS"] = nc.dram_tensor("DNS", [128, DC], F32, kind="ExternalInput")
    dr["OSC"] = nc.dram_tensor("OSC", [128, 1], F32, kind="ExternalInput")
    dr["yT"] = nc.dram_tensor("yT", [D, T], F32, kind="ExternalOutput")
    dr["x1"] = nc.dram_tensor("x1", [D, T], BF16, kind="Internal")
    dr["x2r"] = nc.dram_tensor("x2r", [D, T], BF16, kind="Internal")
    dr["x3"] = nc.dram_tensor("x3", [D, T], BF16, kind="Internal")
    for nm in ("q_scr", "k_scr", "g_scr", "og_scr"):
        dr[nm] = nc.dram_tensor(nm, [4 * D, 512], BF16, kind="Internal")
    dr["r_scr"] = nc.dram_tensor("r_scr", [1, T], F32, kind="Internal")
    dr["h_scr"] = nc.dram_tensor("h_scr", [4 * INTER, 512], BF16,
                                 kind="Internal")
    dr["wdT_scr"] = nc.dram_tensor("wdT_scr", [DC * INTER, 128], BF16,
                                   kind="Internal")
    for nm in ("q1_scr", "k1_scr", "o1_scr"):
        dr[nm] = nc.dram_tensor(nm, [4 * D, 512], BF16, kind="Internal")
    dr["hid_scr"] = nc.dram_tensor("hid_scr", [4 * INTER, 512], BF16,
                                   kind="Internal")
    dr["wd1T_scr"] = nc.dram_tensor("wd1T_scr", [DC * INTER, 128], BF16,
                                    kind="Internal")

    with stack:
        c = make_ctx(nc, stack)
        g = alloc_arenas(nc, stack)
        emit_block_A(c, g, dr)
        barrier(c)
        g.scr_free = [None, None]
        g.piece_free = [None, None]
        emit_block_B(c, g, dr)
        barrier(c)
        g.scr_free = [None, None]
        g.piece_free = [None, None]
        emit_block_C(c, g, dr)
        barrier(c)
        g.scr_free = [None, None]
        g.piece_free = [None, None]
        emit_block_D(c, g, dr)
        c.em.wait("sync", c.dS, c.dS.n)
        c.em.wait("gpsimd", c.dG, c.dG.n)
        c.em.replay()
    return nc



def la_masks():
    m = np.zeros((4 * 128, 512), np.float32)
    kk = np.arange(128)[:, None]
    qq = np.arange(512)[None, :]
    for mi in range(4):
        m[mi * 128:(mi + 1) * 128] = (kk <= qq - 128 * mi)
    return m.astype(bfloat16)


def attn_mask():
    qq = np.arange(128)[:, None]
    kk = np.arange(128)[None, :]
    return np.where(kk <= qq, 0.0, -1e30).astype(np.float32).astype(bfloat16)


def hadamard(n):
    H = np.array([[1.0]], dtype=np.float32)
    while H.shape[0] < n:
        H = np.block([[H, H], [H, -H]]).astype(np.float32)
    return H


def pmajor(v, nchunks):
    return np.ascontiguousarray(np.asarray(v).reshape(nchunks, 128).T)


def kernel(**inputs):
    x = inputs["L0_attn_x"]
    n = 8
    shared = dict(const_inputs())
    shared["Wq"] = inputs["L0_attn_Wq"]
    shared["Wk"] = inputs["L0_attn_Wk"]
    shared["Wv"] = inputs["L0_attn_Wv"]
    shared["Wg"] = inputs["L0_attn_Wg"]
    shared["Wp"] = inputs["L0_attn_Wp"]
    shared["LMASK"] = la_masks()
    shared["Wup"] = inputs["L0_ffn_W_up"]
    shared["Wgt"] = inputs["L0_ffn_W_gate"]
    shared["Wdn"] = inputs["L0_ffn_W_down"]
    shared["H2048"] = hadamard(2048).astype(bfloat16)
    shared["H128"] = hadamard(128).astype(bfloat16)
    shared["SUU"] = pmajor(inputs["L0_ffn_SU_ug"], DC)
    shared["SUD"] = pmajor(inputs["L0_ffn_SU_dp"], IC)
    shared["WSU"] = pmajor(inputs["L0_ffn_Wscale_ug"][:INTER], IC)
    shared["WSG"] = pmajor(inputs["L0_ffn_Wscale_ug"][INTER:], IC)
    shared["WSD"] = pmajor(inputs["L0_ffn_Wscale_dp"], DC)
    shared["Wq1"] = inputs["L1_attn_Wq"]
    shared["Wk1"] = inputs["L1_attn_Wk"]
    shared["Wv1"] = inputs["L1_attn_Wv"]
    shared["Wo1"] = inputs["L1_attn_Wo"]
    shared["AMASK"] = attn_mask()
    shared["Wup1"] = inputs["L1_ffn_up_weight"]
    shared["Wdn1"] = inputs["L1_ffn_down_weight"]
    shared["UPS"] = pmajor(inputs["L1_ffn_up_scale"], IC)
    shared["DNS"] = pmajor(inputs["L1_ffn_down_scale"], DC)
    shared["OSC"] = np.ascontiguousarray(np.broadcast_to(
        np.asarray(inputs["L1_ffn_output_scale"], np.float32).reshape(1, 1),
        (128, 1)))

    in_maps = [{**shared, "xT": np.ascontiguousarray(x[b].T)}
               for b in range(n)]
    nc = build_full()
    res = run_bass_kernel_spmd(nc, in_maps, core_ids=list(range(n)))
    return np.stack([np.asarray(r["yT"], dtype=np.float32).T
                     for r in res.results])


# version 04
# speedup vs baseline: 1.1348x; 1.1348x over previous
import contextlib
import numpy as np
import ml_dtypes
import concourse.bass as bass
import concourse.mybir as mybir
from concourse.bass_utils import run_bass_kernel_spmd

F32 = mybir.dt.float32
BF16 = mybir.dt.bfloat16
AF = mybir.ActivationFunctionType
ALU = mybir.AluOpType
AX = mybir.AxisListType
bfloat16 = ml_dtypes.bfloat16

D = 2048
T = 2048
INTER = 8192
DC = 16
IC = 64
NT = 4
SEG = DC * 512
RMS_EPS = float(np.finfo(np.float32).eps)
LN_EPS = 1e-5
ISQ = 1.0 / float(np.sqrt(128.0))
S1 = 1.0 / (64.0 * float(np.sqrt(2048.0)))
S2 = 1.0 / (64.0 * float(np.sqrt(8192.0)))
ENGINES = ("sync", "gpsimd", "tensor", "vector", "scalar")



class Sem:
    __slots__ = ("h", "n", "name")

    def __init__(self, h, name):
        self.h = h
        self.n = 0
        self.name = name


class Em:

    def __init__(self, nc, stack):
        self.nc = nc
        self.stack = stack
        self.tape = {e: [] for e in ENGINES}
        self.waited = {e: {} for e in ENGINES}
        self.scope = None
        self.scopes = {e: [] for e in ENGINES}

    def sem(self, name):
        h = self.stack.enter_context(self.nc.semaphore(name))
        return Sem(h, name)

    def wait(self, eng, sem, val):
        if sem is None or val is None or val <= 0:
            return
        if self.waited[eng].get(sem.name, 0) >= val:
            return
        self.waited[eng][sem.name] = val
        h = sem.h
        self.tape[eng].append(lambda e, h=h, v=val: e.wait_ge(h, v))
        self.scopes[eng].append(self.scope)

    def wait_evt(self, eng, evt):
        if evt is not None:
            self.wait(eng, evt[0], evt[1])

    def op(self, eng, meth, *args, inc=None, **kw):
        incs = []
        evt = None
        if inc:
            for s, k in inc:
                s.n += k
                incs.append((s.h, k))
                if evt is None:
                    evt = (s, s.n)

        def fn(e, meth=meth, args=args, kw=kw, incs=incs):
            ins = getattr(e, meth)(*args, **kw)
            for h, k in incs:
                ins.then_inc(h, k)
            return ins

        self.tape[eng].append(fn)
        self.scopes[eng].append(self.scope)
        return evt

    def _run(self, e, eng):
        nc = self.nc
        cur = None
        sid = None
        for f, sc in zip(self.tape[eng], self.scopes[eng]):
            if sc != cur:
                if cur is not None:
                    nc.leave_named_scope(cur, sid, False)
                cur = sc
                if cur is not None:
                    sid, _ = nc.enter_named_scope(cur, False)
            f(e)
        if cur is not None:
            nc.leave_named_scope(cur, sid, False)

    def replay(self):
        with self.nc.Block() as blk:
            @blk.sync
            def _(e):
                self._run(e, "sync")

            @blk.gpsimd
            def _(e):
                self._run(e, "gpsimd")

            @blk.tensor
            def _(e):
                self._run(e, "tensor")

            @blk.vector
            def _(e):
                self._run(e, "vector")

            @blk.scalar
            def _(e):
                self._run(e, "scalar")


class Banks:
    def __init__(self, tiles):
        self.tiles = tiles
        self.free = [None] * len(tiles)
        self.idx = 0

    def acquire(self):
        i = self.idx % len(self.tiles)
        self.idx += 1
        return i, self.tiles[i], self.free[i]

    def release(self, i, evt):
        self.free[i] = evt


class Ctx:
    pass


def barrier(c):
    em = c.em
    for eng in ENGINES:
        for s in (c.dG, c.dS, c.tM, c.vV, c.sA, c.pP):
            em.wait(eng, s, s.n)
    c.pstat_free = [None, None]
    c.acc.free = [None] * len(c.acc.tiles)
    c.ptr.free = [None] * len(c.ptr.tiles)


def make_ctx(nc, stack):
    em = Em(nc, stack)
    c = Ctx()
    c.nc = nc
    c.em = em
    c.dG = em.sem("dG")
    c.dS = em.sem("dS")
    c.tM = em.sem("tM")
    c.vV = em.sem("vV")
    c.sA = em.sem("sA")
    c.pP = em.sem("pP")
    c.PB = stack.enter_context(nc.psum_tensor("PB", [128, 2048], F32))
    c.PC = stack.enter_context(nc.psum_tensor("PC", [128, 512], F32))
    c.PD = stack.enter_context(nc.psum_tensor("PD", [128, 512], F32))
    c.acc = Banks([c.PB[:, i * 512:(i + 1) * 512] for i in range(4)])
    c.ptr = Banks([stack.enter_context(nc.psum_tensor(f"ptr{i}", [128, 1024], BF16))
                   for i in range(2)])
    c.pstat = [c.PD[0:1, 0:512], c.PC[0:1, 0:512]]
    c.pstat_free = [None, None]
    c.idn = stack.enter_context(nc.sbuf_tensor("idn", [128, 128], BF16))
    c.ones_col = stack.enter_context(nc.sbuf_tensor("ones_col", [128, 1], BF16))
    c.ones_row = stack.enter_context(nc.sbuf_tensor("ones_row", [1, 128], BF16))
    c.epsc = stack.enter_context(nc.sbuf_tensor("epsc", [128, 2], F32))
    eps_d = nc.dram_tensor("EPSC", [128, 2], F32, kind="ExternalInput")
    idn_d = nc.dram_tensor("IDN", [128, 128], BF16, kind="ExternalInput")
    onc_d = nc.dram_tensor("ONESC", [128, 1], BF16, kind="ExternalInput")
    onr_d = nc.dram_tensor("ONESR", [1, 128], BF16, kind="ExternalInput")
    em.op("sync", "dma_start", c.epsc[:, :], eps_d[:, :], inc=[(c.dS, 16)])
    em.op("sync", "dma_start", c.idn[:, :], idn_d[:, :], inc=[(c.dS, 16)])
    em.op("sync", "dma_start", c.ones_col[:, :], onc_d[:, :], inc=[(c.dS, 16)])
    c.const_evt = em.op("sync", "dma_start", c.ones_row[:, :], onr_d[:, :],
                        inc=[(c.dS, 16)])
    return c


def const_inputs():
    return {
        "EPSC": np.tile(np.array([[RMS_EPS, LN_EPS]], np.float32), (128, 1)),
        "IDN": np.eye(128, dtype=np.float32).astype(bfloat16),
        "ONESC": np.ones((128, 1), np.float32).astype(bfloat16),
        "ONESR": np.ones((1, 128), np.float32).astype(bfloat16),
    }



def emit_wt_load(c, W_dram, wt_sb, wt_off, natbufs, nat_free, n_oslabs,
                 d_free, wt_free_evt=None):
    em = c.em
    nk = d_free // 128
    slab_ready = []
    for s in range(n_oslabs):
        nbi = s % len(natbufs)
        nb = natbufs[nbi]
        em.wait_evt("gpsimd", nat_free[nbi])
        ld = em.op("gpsimd", "dma_start", nb[:, 0:d_free],
                   W_dram[s * 128:(s + 1) * 128, 0:d_free], inc=[(c.dG, 16)])
        last_evac = None
        tev = None
        for g in range(nk // 4):
            bi, bt, bfree = c.ptr.acquire()
            em.wait_evt("tensor", ld)
            em.wait_evt("tensor", bfree)
            for i in range(4):
                k = g * 4 + i
                inc = [(c.tM, 1)] if i == 3 else None
                tev = em.op("tensor", "transpose", bt[:, i * 128:(i + 1) * 128],
                            nb[:, k * 128:(k + 1) * 128], c.idn[:, :], inc=inc)
            em.wait_evt("vector", tev)
            em.wait_evt("vector", wt_free_evt)
            off = wt_off + s * d_free + g * 4 * 128
            last_evac = em.op("vector", "tensor_copy",
                              wt_sb[:, off:off + 512], bt[:, 0:512],
                              inc=[(c.vV, 1)])
            c.ptr.release(bi, last_evac)
        nat_free[nbi] = tev
        slab_ready.append(last_evac)
    return slab_ready



def emit_stats_deferred(c, Y, ye_last, sq16, sqe_list, prev_stats):
    em = c.em
    em.wait_evt("tensor", c.pstat_free[0])
    em.wait_evt("tensor", c.pstat_free[1])
    em.wait_evt("tensor", ye_last)
    em.wait_evt("tensor", sqe_list[-1])
    ev = None
    for o in range(DC):
        em.op("tensor", "matmul", c.pstat[0][:, :], c.ones_col[:, :],
              Y[:, o * 512:(o + 1) * 512], start=(o == 0), stop=(o == DC - 1))
        ev = em.op("tensor", "matmul", c.pstat[1][:, :], c.ones_col[:, :],
                   sq16[:, o * 512:(o + 1) * 512], start=(o == 0),
                   stop=(o == DC - 1), inc=([(c.tM, 1)] if o == DC - 1 else None))
    return ev


def emit_sq(c, y_ap, y_evt, sq_slice, sq_free):
    em = c.em
    em.wait_evt("scalar", y_evt)
    em.wait_evt("scalar", sq_free)
    return em.op("scalar", "activation", sq_slice, y_ap, AF.Square,
                 inc=[(c.sA, 1)])


def emit_stats_mms(c, y_ap, y_evt, sq_buf, sq_free, first, last):
    em = c.em
    em.wait_evt("scalar", y_evt)
    em.wait_evt("scalar", sq_free)
    sqe = em.op("scalar", "activation", sq_buf, y_ap, AF.Square,
                inc=[(c.sA, 1)])
    if first:
        em.wait_evt("tensor", c.pstat_free[0])
        em.wait_evt("tensor", c.pstat_free[1])
    em.wait_evt("tensor", y_evt)
    em.op("tensor", "matmul", c.pstat[0][:, :], c.ones_col[:, :], y_ap,
          start=first, stop=last)
    em.wait_evt("tensor", sqe)
    ev = em.op("tensor", "matmul", c.pstat[1][:, :], c.ones_col[:, :],
               sq_buf, start=first, stop=last, inc=[(c.tM, 1)])
    return ev, sqe


def emit_ln_finalize(c, s, stats_evt):
    em = c.em
    em.wait_evt("vector", stats_evt)
    em.op("vector", "tensor_scalar", s["m"], c.pstat[0][:, :], 1.0 / D, None,
          ALU.mult)
    em.op("vector", "tensor_tensor", s["t0"], s["m"], s["m"], ALU.mult)
    ev_var = em.op("vector", "scalar_tensor_tensor", s["t1"], c.pstat[1][:, :],
                   1.0 / D, s["t0"], ALU.mult, ALU.subtract, inc=[(c.vV, 1)])
    c.pstat_free[0] = ev_var
    c.pstat_free[1] = ev_var
    em.wait_evt("scalar", ev_var)
    em.op("scalar", "activation", s["t0"], s["t1"], AF.Ln,
          bias=c.epsc[0:1, 1:2])
    ev_r = em.op("scalar", "activation", s["t1"], s["t0"], AF.Exp,
                 scale=-0.5, inc=[(c.sA, 1)])
    em.wait_evt("vector", ev_r)
    evA = em.op("vector", "tensor_copy", s["row_a"], s["t1"], inc=[(c.vV, 1)])
    em.op("vector", "scalar_tensor_tensor", s["t0"], s["m"], -1.0, s["t1"],
          ALU.mult, ALU.mult)
    evB = em.op("vector", "tensor_copy", s["row_b"], s["t0"], inc=[(c.vV, 1)])
    iA, Ab, fA = c.acc.acquire()
    em.wait_evt("tensor", fA)
    em.wait_evt("tensor", evA)
    evtA = em.op("tensor", "matmul", Ab[:, :], c.ones_row[:, :], s["row_a"],
                 start=True, stop=True, inc=[(c.tM, 1)])
    iB, Bb, fB = c.acc.acquire()
    em.wait_evt("tensor", fB)
    em.wait_evt("tensor", evB)
    evtB = em.op("tensor", "matmul", Bb[:, :], c.ones_row[:, :], s["row_b"],
                 start=True, stop=True, inc=[(c.tM, 1)])
    return (iA, Ab, evtA), (iB, Bb, evtB)


def emit_ln_apply(c, g, Y, out_writer, A_t, B_t):
    em = c.em
    iA, Ab, evtA = A_t
    iB, Bb, evtB = B_t
    pe = None
    for o in range(DC):
        em.wait_evt("vector", evtA)
        em.wait_evt("vector", evtB)
        si = o % 2
        em.wait_evt("vector", g.scr_free[si])
        em.op("vector", "tensor_tensor", g.scr[si][:, :],
              Y[:, o * 512:(o + 1) * 512], Ab[:, :], ALU.mult)
        pe = out_writer(o, g.scr[si])
        g.scr_free[si] = pe
    c.acc.release(iA, pe)
    c.acc.release(iB, pe)
    return pe


class G:
    pass


def alloc_arenas(nc, stack):
    g = G()
    g.G1 = stack.enter_context(nc.sbuf_tensor("G1", [128, DC * D], BF16))
    g.G2 = stack.enter_context(nc.sbuf_tensor("G2", [128, DC * D], BF16))
    g.G3 = stack.enter_context(nc.sbuf_tensor("G3", [128, 2 * SEG], BF16))
    g.G4 = stack.enter_context(nc.sbuf_tensor("G4", [128, SEG], BF16))
    g.FA = stack.enter_context(nc.sbuf_tensor("FA", [128, 5248], F32))
    g.BA = stack.enter_context(nc.sbuf_tensor("BA", [128, 5376], BF16))
    g.scr = [g.FA[:, 0:512], g.FA[:, 512:1024]]
    g.ln = {
        "m": g.FA[0:1, 1024:1536], "t0": g.FA[0:1, 1536:2048],
        "t1": g.FA[0:1, 2048:2560],
    }
    g.S_sb = g.FA[:, 2560:4608]
    g.rowb = g.FA[:, 4608:4640]
    g.vecsB = g.FA[:, 4640:5152]
    g.vecsD = g.FA[:, 5152:5248]
    g.piece = [g.BA[:, i * 512:(i + 1) * 512] for i in range(4)]
    g.ln["row_a"] = g.BA[0:1, 2048:2560]
    g.ln["row_b"] = g.BA[0:1, 2560:3072]
    g.h128 = g.BA[:, 3072:3200]
    g.amask = g.BA[:, 3200:3328]
    g.og = [g.BA[:, 3328:3840], g.BA[:, 3840:4352]]
    g.x3p = [g.BA[:, 4352:4864], g.BA[:, 4864:5376]]
    g.lmask_off = None
    g.scr_free = [None, None]
    g.piece_free = [None, None, None, None]
    return g



def emit_block_A(c, g, dr):
    em = c.em
    nc = c.nc
    xT, Wq, Wk, Wv, Wg, Wp, LMASK, x1_out = (
        dr["xT"], dr["Wq"], dr["Wk"], dr["Wv"], dr["Wg"], dr["Wp"],
        dr["LMASK"], dr["x1"])
    q_scr, k_scr, g_scr, og_scr, r_scr = (
        dr["q_scr"], dr["k_scr"], dr["g_scr"], dr["og_scr"], dr["r_scr"])

    AR_W = g.G1
    AR_V = g.G2
    xblk = [g.G3[:, 0:SEG], g.G3[:, SEG:2 * SEG]]
    r_bc = g.G4[:, 0:2048]
    nat = [g.G4[:, 2048 + i * 2048:2048 + (i + 1) * 2048] for i in range(2)]
    nat3 = [g.G2[:, 512 + i * 2048:512 + (i + 1) * 2048] for i in range(2)]
    r_tok = g.FA[:, 4640:4656]
    sq = g.G2[:, 0:512]
    scr = g.scr
    piece = g.piece
    ln_sc = g.ln
    lmask = AR_W[:, 3 * SEG:3 * SEG + 2048]

    xblk_free = [None, None]
    nat_free = [None, None]
    scr_free = g.scr_free
    piece_free = g.piece_free

    xT_re = xT[:, :].rearrange("(c p) n -> p c n", p=128)

    def load_xblk(nt, slot):
        em.wait_evt("gpsimd", xblk_free[slot])
        src = xT_re[:, :, nt * 512:(nt + 1) * 512]
        dst = xblk[slot].rearrange("p (c n) -> p c n", c=DC)
        return em.op("gpsimd", "dma_start", dst, src, inc=[(c.dG, 16)])

    em.scope = "A0"
    em.wait_evt("tensor", c.const_evt)
    for nt in range(NT):
        lev = load_xblk(nt, nt % 2)
        sqe = None
        for ch in range(DC):
            ya = xblk[nt % 2][:, ch * 512:(ch + 1) * 512]
            em.wait_evt("scalar", lev)
            em.wait_evt("scalar", sqe)
            sqe = em.op("scalar", "activation", sq, ya, AF.Square,
                        inc=[(c.sA, 1)])
            if ch == 0:
                em.wait_evt("tensor", c.pstat_free[1])
            em.wait_evt("tensor", sqe)
            sev = em.op("tensor", "matmul", c.pstat[1][:, :], c.ones_col[:, :],
                        sq, start=(ch == 0), stop=(ch == DC - 1),
                        inc=[(c.tM, 1)])
        xblk_free[nt % 2] = sqe
        em.wait_evt("vector", sev)
        ev_var = em.op("vector", "tensor_scalar", ln_sc["t1"],
                       c.pstat[1][:, :], 1.0 / D, None, ALU.mult,
                       inc=[(c.vV, 1)])
        c.pstat_free[1] = ev_var
        em.wait_evt("scalar", ev_var)
        em.op("scalar", "activation", ln_sc["t0"], ln_sc["t1"], AF.Ln,
              bias=c.epsc[0:1, 0:1])
        ev_r = em.op("scalar", "activation", ln_sc["t1"], ln_sc["t0"], AF.Exp,
                     scale=-0.5, inc=[(c.sA, 1)])
        em.wait_evt("vector", ev_r)
        evA = em.op("vector", "tensor_copy", ln_sc["row_a"], ln_sc["t1"],
                    inc=[(c.vV, 1)])
        em.wait_evt("sync", ev_r)
        em.op("sync", "dma_start", r_scr[0:1, nt * 512:(nt + 1) * 512],
              ln_sc["t1"], inc=[(c.dS, 16)])
        ia, tA, fA = c.acc.acquire()
        em.wait_evt("tensor", fA)
        em.wait_evt("tensor", evA)
        bev = em.op("tensor", "matmul", tA[:, :], c.ones_row[:, :],
                    ln_sc["row_a"], start=True, stop=True, inc=[(c.tM, 1)])
        em.wait_evt("vector", bev)
        cev = em.op("vector", "tensor_copy", r_bc[:, nt * 512:(nt + 1) * 512],
                    tA[:, :], inc=[(c.vV, 1)])
        c.acc.release(ia, cev)
    r_rows_done = (c.dS, c.dS.n)
    em.wait("gpsimd", *r_rows_done)
    rt_src = r_scr[:, :].rearrange("one (cc p) -> p (one cc)", p=128)
    rt_ev = em.op("gpsimd", "dma_start", r_tok, rt_src, inc=[(c.dG, 16)],
                  allow_slow_non_contiguous=True)

    em.scope = "A1"
    def proj_standard(wt_view, slab_ready, spill, kind):
        pi = [0]
        mme = None
        for nt in range(NT):
            lev = load_xblk(nt, nt % 2)
            for o in range(DC):
                ia, acc, fr = c.acc.acquire()
                em.wait_evt("tensor", fr)
                em.wait_evt("tensor", slab_ready[o])
                em.wait_evt("tensor", lev)
                for k in range(DC):
                    mme = em.op(
                        "tensor", "matmul", acc[:, :],
                        wt_view[:, o * D + k * 128:o * D + (k + 1) * 128],
                        xblk[nt % 2][:, k * 512:(k + 1) * 512],
                        start=(k == 0), stop=(k == DC - 1),
                        inc=([(c.tM, 1)] if k == DC - 1 else None))
                em.wait_evt("vector", mme)
                si = o % 2
                s0 = scr[si]
                em.wait_evt("vector", scr_free[si])
                rv = em.op("vector", "tensor_tensor", s0[:, :], acc[:, :],
                           r_bc[:, nt * 512:(nt + 1) * 512], ALU.mult,
                           inc=[(c.vV, 1)])
                c.acc.release(ia, rv)
                p = pi[0] % 2
                pc = piece[p]
                if kind == "silu":
                    em.wait_evt("scalar", rv)
                    em.wait_evt("scalar", piece_free[p])
                    pe = em.op("scalar", "activation", pc[:, :], s0[:, :],
                               AF.Silu, inc=[(c.sA, 1)])
                    scr_free[si] = pe
                else:
                    sit = (o + 1) % 2
                    em.wait_evt("scalar", rv)
                    em.wait_evt("scalar", scr_free[sit])
                    th = em.op("scalar", "activation", scr[sit][:, :],
                               s0[:, :], AF.Tanh, scale=0.5, inc=[(c.sA, 1)])
                    scr_free[si] = th
                    em.wait_evt("vector", th)
                    em.wait_evt("vector", piece_free[p])
                    pe = em.op("vector", "tensor_scalar", pc[:, :],
                               scr[sit][:, :], 0.5, 0.5, ALU.mult, ALU.add,
                               inc=[(c.vV, 1)])
                    scr_free[sit] = pe
                em.wait_evt("sync", pe)
                de = em.op("sync", "dma_start",
                           spill[nt * D + o * 128:nt * D + (o + 1) * 128, :],
                           pc[:, :], inc=[(c.dS, 16)])
                piece_free[p] = de
                pi[0] += 1
            xblk_free[nt % 2] = mme
        return mme

    wt_free = None
    sr = emit_wt_load(c, Wq, AR_W, 0, nat, nat_free, DC, D, wt_free)
    wt_free = proj_standard(AR_W, sr, q_scr, "silu")
    sr = emit_wt_load(c, Wk, AR_W, 0, nat, nat_free, DC, D, wt_free)
    wt_free = proj_standard(AR_W, sr, k_scr, "silu")
    sr = emit_wt_load(c, Wg, AR_W, 0, nat, nat_free, DC, D, wt_free)
    wt_free = proj_standard(AR_W, sr, g_scr, "sigmoid")
    sr = emit_wt_load(c, Wv, AR_W, 0, nat, nat_free, DC, D, wt_free)

    em.wait_evt("vector", rt_ev)
    wtv_re = AR_W[:, :].rearrange("p (s x) -> p s x", s=DC)
    for m in range(4):
        for nb in range(NT):
            lev = load_xblk(nb, nb % 2)
            for ns in range(4):
                nch = nb * 4 + ns
                ia, acc, fr = c.acc.acquire()
                em.wait_evt("tensor", fr)
                em.wait_evt("tensor", sr[4 * m + 3])
                em.wait_evt("tensor", lev)
                for k in range(DC):
                    rhs = wtv_re[:, 4 * m:4 * m + 4, k * 128:(k + 1) * 128]
                    mme = em.op(
                        "tensor", "matmul", acc[:, :],
                        xblk[nb % 2][:, k * 512 + ns * 128:k * 512 + (ns + 1) * 128],
                        rhs, start=(k == 0), stop=(k == DC - 1),
                        inc=([(c.tM, 1)] if k == DC - 1 else None))
                em.wait_evt("vector", mme)
                si = ns % 2
                em.wait_evt("vector", scr_free[si])
                rv = em.op("vector", "tensor_scalar", scr[si][:, :], acc[:, :],
                           r_tok[:, nch:nch + 1], None, ALU.mult,
                           inc=[(c.vV, 1)])
                c.acc.release(ia, rv)
                em.wait_evt("scalar", rv)
                ve = em.op("scalar", "activation",
                           AR_V[:, nch * D + m * 512:nch * D + (m + 1) * 512],
                           scr[si][:, :], AF.Silu, inc=[(c.sA, 1)])
                scr_free[si] = ve
            xblk_free[nb % 2] = mme
    V_ready = (c.sA, c.sA.n)
    a1_t_done = (c.tM, c.tM.n)
    a1_v_done = (c.vV, c.vV.n)

    em.scope = "A2"
    lm_dst = lmask.rearrange("p (m n) -> p m n", m=4)
    lm_src = LMASK[:, :].rearrange("(m p) n -> p m n", p=128)
    em.wait_evt("sync", a1_t_done)
    lm_ev = em.op("sync", "dma_start", lm_dst, lm_src, inc=[(c.dS, 16)])
    a1_spill_done = (c.dS, c.dS.n)

    kblk = [AR_W[:, 0:SEG], AR_W[:, SEG:2 * SEG]]
    S_sb = AR_W[:, 2 * SEG:3 * SEG]
    qblk = xblk
    gblk = g.G4[:, 0:SEG]
    og_sb = g.og

    def load_blk(scr_d, dst, t, ev_w, eng, sem):
        em.wait_evt(eng, ev_w)
        src = scr_d[t * D:(t + 1) * D, :].rearrange("(c p) n -> p c n", p=128)
        dst3 = dst.rearrange("p (c n) -> p c n", c=DC)
        return em.op(eng, "dma_start", dst3, src, inc=[(sem, 16)])

    kblk_free = [None, None]
    gblk_free = None
    og_free = [None, None]
    qblk_free = [xblk_free[0], xblk_free[1]]
    Ssb_free = None
    ogi = 0
    em.wait_evt("gpsimd", a1_t_done)
    em.wait_evt("gpsimd", a1_v_done)
    for t in range(4):
        em.wait("sync", *a1_spill_done)
        em.wait("gpsimd", *a1_spill_done)
        qlev = load_blk(q_scr, qblk[t % 2], t, qblk_free[t % 2], "sync", c.dS)
        glev = load_blk(g_scr, gblk, t, gblk_free, "gpsimd", c.dG)
        nkc = 4 * t + 4
        ev_s_seg = [None] * nkc
        for u in range(t + 1):
            klev = load_blk(k_scr, kblk[u % 2], u, kblk_free[u % 2],
                            "sync", c.dS)
            for jj in range(4):
                j = 4 * u + jj
                ia, acc, fr = c.acc.acquire()
                em.wait_evt("tensor", fr)
                em.wait_evt("tensor", klev)
                em.wait_evt("tensor", qlev)
                for k in range(DC):
                    mme = em.op(
                        "tensor", "matmul", acc[:, :],
                        kblk[u % 2][:, k * 512 + jj * 128:k * 512 + (jj + 1) * 128],
                        qblk[t % 2][:, k * 512:(k + 1) * 512],
                        start=(k == 0), stop=(k == DC - 1),
                        inc=([(c.tM, 1)] if k == DC - 1 else None))
                em.wait_evt("vector", mme)
                em.wait_evt("vector", Ssb_free)
                if u == t:
                    em.wait_evt("vector", lm_ev)
                    me = em.op("vector", "tensor_tensor",
                               S_sb[:, j * 512:(j + 1) * 512], acc[:, :],
                               lmask[:, jj * 512:(jj + 1) * 512], ALU.mult,
                               inc=[(c.vV, 1)])
                else:
                    me = em.op("vector", "tensor_copy",
                               S_sb[:, j * 512:(j + 1) * 512], acc[:, :],
                               inc=[(c.vV, 1)])
                c.acc.release(ia, me)
                ev_s_seg[j] = me
            kblk_free[u % 2] = mme
        for dch in range(DC):
            ia, acc, fr = c.acc.acquire()
            em.wait_evt("tensor", fr)
            em.wait_evt("tensor", V_ready)
            for j in range(nkc):
                em.wait_evt("tensor", ev_s_seg[j])
                mme = em.op("tensor", "matmul", acc[:, :],
                            AR_V[:, j * D + dch * 128:j * D + (dch + 1) * 128],
                            S_sb[:, j * 512:(j + 1) * 512],
                            start=(j == 0), stop=(j == nkc - 1),
                            inc=([(c.tM, 1)] if j == nkc - 1 else None))
            em.wait_evt("vector", mme)
            em.wait_evt("vector", glev)
            em.wait_evt("vector", og_free[ogi % 2])
            ge = em.op("vector", "tensor_tensor", og_sb[ogi % 2][:, :],
                       acc[:, :], gblk[:, dch * 512:(dch + 1) * 512],
                       ALU.mult, inc=[(c.vV, 1)])
            c.acc.release(ia, ge)
            em.wait_evt("sync", ge)
            og_free[ogi % 2] = em.op(
                "sync", "dma_start",
                og_scr[t * D + dch * 128:t * D + (dch + 1) * 128, :],
                og_sb[ogi % 2][:, :], inc=[(c.dS, 16)])
            ogi += 1
        qblk_free[t % 2] = mme
        gblk_free = ge
        Ssb_free = mme
    xblk_free[0] = qblk_free[0]
    xblk_free[1] = qblk_free[1]

    em.scope = "A3"
    a2_done = (c.dS, c.dS.n)
    nat_free3 = [V_ready, V_ready]
    em.wait_evt("gpsimd", Ssb_free)
    sr = emit_wt_load(c, Wp, AR_W, 0, nat3, nat_free3, DC, D, Ssb_free)
    ogblk = xblk[0]
    xres = xblk[1]
    Y = g.G4[:, 0:SEG]
    sq16 = AR_V[:, 8192:16384]
    sq16_free = Ssb_free
    ogblk_free = qblk_free[0]
    xres_free = qblk_free[1]
    Y_free = gblk_free
    for nt in range(NT):
        em.wait("sync", *a2_done)
        olev = load_blk(og_scr, ogblk, nt, ogblk_free, "sync", c.dS)
        em.wait_evt("gpsimd", xres_free)
        xr_dst = xres.rearrange("p (cc n) -> p cc n", cc=DC)
        xlev = em.op("gpsimd", "dma_start", xr_dst,
                     xT_re[:, :, nt * 512:(nt + 1) * 512], inc=[(c.dG, 16)])
        sq_evts = []
        for o in range(DC):
            ia, acc, fr = c.acc.acquire()
            em.wait_evt("tensor", fr)
            em.wait_evt("tensor", sr[o])
            em.wait_evt("tensor", olev)
            for k in range(DC):
                mme = em.op("tensor", "matmul", acc[:, :],
                            AR_W[:, o * D + k * 128:o * D + (k + 1) * 128],
                            ogblk[:, k * 512:(k + 1) * 512],
                            start=(k == 0), stop=(k == DC - 1),
                            inc=([(c.tM, 1)] if k == DC - 1 else None))
            em.wait_evt("vector", mme)
            em.wait_evt("vector", xlev)
            em.wait_evt("vector", Y_free)
            ye = em.op("vector", "tensor_tensor", Y[:, o * 512:(o + 1) * 512],
                       acc[:, :], xres[:, o * 512:(o + 1) * 512], ALU.add,
                       inc=[(c.vV, 1)])
            c.acc.release(ia, ye)
            sq_evts.append(emit_sq(c, Y[:, o * 512:(o + 1) * 512], ye,
                                   sq16[:, o * 512:(o + 1) * 512], sq16_free))
        ogblk_free = mme
        xres_free = ye
        stats_evt = emit_stats_deferred(c, Y, ye, sq16, sq_evts, None)
        sq16_free = stats_evt
        A_t, B_t = emit_ln_finalize(c, ln_sc, stats_evt)

        def writer(o, s0, nt=nt):
            p = o % 2
            em.wait_evt("vector", piece_free[p])
            pe = em.op("vector", "tensor_tensor", piece[p][:, :], s0[:, :],
                       B_t[1][:, :], ALU.add, inc=[(c.vV, 1)])
            em.wait_evt("sync", pe)
            de = em.op("sync", "dma_start",
                       x1_out[o * 128:(o + 1) * 128, nt * 512:(nt + 1) * 512],
                       piece[p][:, :], inc=[(c.dS, 16)])
            piece_free[p] = de
            return pe

        Y_free = emit_ln_apply(c, g, Y, writer, A_t, B_t)



def emit_block_B(c, g, dr):
    em = c.em
    x1 = dr["x1"]
    Wup, Wgt, Wdn = dr["Wup"], dr["Wgt"], dr["Wdn"]
    H2048, H128 = dr["H2048"], dr["H128"]
    SUU, SUD, WSU, WSG, WSD = dr["SUU"], dr["SUD"], dr["WSU"], dr["WSG"], dr["WSD"]
    x2r_out = dr["x2r"]
    h_scr, wdT_scr = dr["h_scr"], dr["wdT_scr"]

    AR_H = g.G1
    AR_X2 = g.G2
    AR_X = g.G3
    vecs = g.vecsB
    suu_sb = vecs[:, 0:DC]
    wsd64 = vecs[:, DC:2 * DC]
    wsu_su = g.S_sb[:, 0:IC]
    wsg64 = g.S_sb[:, IC:2 * IC]
    sud_sb = g.S_sb[:, 2 * IC:3 * IC]
    wsu_sb = g.S_sb[:, 3 * IC:4 * IC]
    wsg_sb = g.S_sb[:, 4 * IC:5 * IC]
    wsd_sb = g.S_sb[:, 5 * IC:5 * IC + DC]
    scr = g.scr
    piece = g.piece
    ln_sc = g.ln
    h128_sb = g.h128
    sq = AR_X2[:, 25600:26112]
    scr_free = g.scr_free
    piece_free = g.piece_free

    em.scope = "B0"
    for v_sb, v_d in ((suu_sb, SUU), (sud_sb, dr["SUD"]), (wsu_sb, WSU),
                      (wsg_sb, WSG), (wsd_sb, WSD)):
        em.op("sync", "dma_start", v_sb, v_d[:, :], inc=[(c.dS, 16)])
    em.op("sync", "dma_start", h128_sb, H128[:, :], inc=[(c.dS, 16)])
    em.wait("vector", c.dS, c.dS.n)
    em.op("vector", "tensor_tensor", wsu_su, wsu_sb, sud_sb, ALU.mult)
    vv1 = em.op("vector", "tensor_scalar", wsu_su, wsu_su, 64.0, None,
                ALU.mult, inc=[(c.vV, 1)])
    vv2 = em.op("vector", "tensor_scalar", wsg64, wsg_sb, 64.0, None,
                ALU.mult, inc=[(c.vV, 1)])
    vv3 = em.op("vector", "tensor_scalar", wsd64, wsd_sb, 64.0 * S2, None,
                ALU.mult, inc=[(c.vV, 1)])
    H_re = H2048[:, :].rearrange("(c p) n -> p c n", p=128)
    hlev = em.op("sync", "dma_start",
                 AR_H[:, :].rearrange("p (c n) -> p c n", c=DC), H_re,
                 inc=[(c.dS, 16)])

    em.scope = "B1"
    x1blk = AR_X[:, 0:SEG]
    xsblk = AR_X[:, SEG:2 * SEG]
    x1_re = x1[:, :].rearrange("(c p) n -> p c n", p=128)
    x1blk_free = None
    xs_free = None
    mme = None
    for nt in range(NT):
        em.wait_evt("gpsimd", x1blk_free)
        lev = em.op("gpsimd", "dma_start",
                    x1blk.rearrange("p (c n) -> p c n", c=DC),
                    x1_re[:, :, nt * 512:(nt + 1) * 512], inc=[(c.dG, 16)])
        sev = None
        for ch in range(DC):
            em.wait_evt("vector", lev)
            em.wait_evt("vector", xs_free)
            sev = em.op("vector", "tensor_scalar",
                        xsblk[:, ch * 512:(ch + 1) * 512],
                        x1blk[:, ch * 512:(ch + 1) * 512],
                        suu_sb[:, ch:ch + 1], None, ALU.mult,
                        inc=[(c.vV, 1)])
        x1blk_free = sev
        for o in range(DC):
            ia, acc, fr = c.acc.acquire()
            em.wait_evt("tensor", fr)
            em.wait_evt("tensor", hlev)
            em.wait_evt("tensor", sev)
            for k in range(DC):
                mme = em.op("tensor", "matmul", acc[:, :],
                            AR_H[:, k * D + o * 128:k * D + (o + 1) * 128],
                            xsblk[:, k * 512:(k + 1) * 512],
                            start=(k == 0), stop=(k == DC - 1),
                            inc=([(c.tM, 1)] if k == DC - 1 else None))
            em.wait_evt("scalar", mme)
            ce = em.op("scalar", "activation",
                       AR_X2[:, o * D + nt * 512:o * D + (nt + 1) * 512],
                       acc[:, :], AF.Copy, scale=S1, inc=[(c.sA, 1)])
            c.acc.release(ia, ce)
        xs_free = mme
    x2_ready = (c.sA, c.sA.n)
    ar_h_free = mme

    em.scope = "B2"
    natU = [AR_X[:, i * D:(i + 1) * D] for i in range(2)]
    natG = [AR_X[:, (2 + i) * D:(3 + i) * D] for i in range(2)]
    wtU = [AR_X[:, (4 + i) * D:(5 + i) * D] for i in range(2)]
    wtG = [AR_X[:, (6 + i) * D:(7 + i) * D] for i in range(2)]
    natU_free = [x1blk_free, x1blk_free]
    natG_free = [x1blk_free, x1blk_free]
    wtU_free = [xs_free, xs_free]
    wtG_free = [xs_free, xs_free]
    nat8 = [AR_H[:, 0:INTER], AR_H[:, INTER:2 * INTER]]
    nat8_free = [ar_h_free, ar_h_free]
    slabbuf = g.G4[:, 0:INTER // 2 * 2]
    slabbuf_free = [None]

    prep_bev = [None]

    def prep_part1(s):
        sl = s % 2
        em.wait_evt("gpsimd", nat8_free[sl])
        ld = em.op("gpsimd", "dma_start", nat8[sl],
                   Wdn[s * 128:(s + 1) * 128, :], inc=[(c.dG, 16)])
        tev = None
        sv = None
        for gg in range(16):
            bi, bt, bfree = c.ptr.acquire()
            em.wait_evt("tensor", ld)
            em.wait_evt("tensor", bfree)
            for i in range(4):
                k = gg * 4 + i
                inc = [(c.tM, 1)] if i == 3 else None
                tev = em.op("tensor", "transpose",
                            bt[:, i * 128:(i + 1) * 128],
                            nat8[sl][:, k * 128:(k + 1) * 128],
                            c.idn[:, :], inc=inc)
            em.wait_evt("vector", tev)
            em.wait_evt("vector", slabbuf_free[0])
            sv = em.op("vector", "tensor_copy",
                       slabbuf[:, gg * 512:(gg + 1) * 512], bt[:, 0:512],
                       inc=[(c.vV, 1)])
            c.ptr.release(bi, sv)
        nat8_free[sl] = tev
        hh = 1
        bev = sv
        while hh < IC:
            v5 = slabbuf.rearrange("p (b t j o) -> p b t j o",
                                   b=IC // (2 * hh), t=2, j=hh)
            a_ap = v5[:, :, 0, :, :]
            b_ap = v5[:, :, 1, :, :]
            em.op("vector", "tensor_sub", b_ap, a_ap, b_ap)
            bev = em.op("vector", "scalar_tensor_tensor", a_ap, a_ap, 2.0,
                        b_ap, ALU.mult, ALU.subtract, inc=[(c.vV, 1)])
            hh *= 2
        prep_bev[0] = bev

    def prep_part2(s):
        bev = prep_bev[0]
        hv = None
        for gg in range(16):
            ia, acc, fr = c.acc.acquire()
            em.wait_evt("tensor", fr)
            em.wait_evt("tensor", bev)
            for i in range(4):
                cc2 = gg * 4 + i
                inc = [(c.tM, 1)] if i == 3 else None
                hm = em.op("tensor", "matmul", acc[:, i * 128:(i + 1) * 128],
                           h128_sb, slabbuf[:, cc2 * 128:(cc2 + 1) * 128],
                           start=True, stop=True, inc=inc)
            em.wait_evt("vector", hm)
            hv = em.op("vector", "tensor_copy",
                       slabbuf[:, gg * 512:(gg + 1) * 512], acc[:, 0:512],
                       inc=[(c.vV, 1)])
            c.acc.release(ia, hv)
        em.wait_evt("sync", hv)
        dst = wdT_scr[s * INTER:(s + 1) * INTER, :]
        dst3 = dst.rearrange("(cc i) o -> i cc o", i=128)
        src3 = slabbuf.rearrange("i (cc o) -> i cc o", cc=IC)
        de = em.op("sync", "dma_start", dst3, src3, inc=[(c.dS, 16)])
        slabbuf_free[0] = de

    def slab_load(W_dram, cc, natb, nat_free_l, slot, wt_dst, wt_free_l):
        em.wait_evt("gpsimd", nat_free_l[slot])
        ld = em.op("gpsimd", "dma_start", natb[slot],
                   W_dram[cc * 128:(cc + 1) * 128, :], inc=[(c.dG, 16)])
        tev = None
        evac = None
        for gg in range(4):
            bi, bt, bfree = c.ptr.acquire()
            em.wait_evt("tensor", ld)
            em.wait_evt("tensor", bfree)
            for i in range(4):
                k = gg * 4 + i
                inc = [(c.tM, 1)] if i == 3 else None
                tev = em.op("tensor", "transpose",
                            bt[:, i * 128:(i + 1) * 128],
                            natb[slot][:, k * 128:(k + 1) * 128],
                            c.idn[:, :], inc=inc)
            em.wait_evt("vector", tev)
            em.wait_evt("vector", wt_free_l[slot])
            evac = em.op("vector", "tensor_copy",
                         wt_dst[slot][:, gg * 512:(gg + 1) * 512],
                         bt[:, 0:512], inc=[(c.vV, 1)])
            c.ptr.release(bi, evac)
        nat_free_l[slot] = tev
        return evac

    hp_free = [None, None, None, None]
    mmu = mmg = None
    uev = slab_load(Wup, 0, natU, natU_free, 0, wtU, wtU_free)
    gev = slab_load(Wgt, 0, natG, natG_free, 0, wtG, wtG_free)
    for cc in range(IC):
        sl = cc % 2
        if cc + 1 < IC:
            nxt_uev = slab_load(Wup, cc + 1, natU, natU_free, (cc + 1) % 2,
                                wtU, wtU_free)
            nxt_gev = slab_load(Wgt, cc + 1, natG, natG_free, (cc + 1) % 2,
                                wtG, wtG_free)
        for nt in range(NT):
            iu, pu, fu = c.acc.acquire()
            em.wait_evt("tensor", fu)
            em.wait_evt("tensor", uev)
            em.wait_evt("tensor", x2_ready)
            for k in range(DC):
                mmu = em.op("tensor", "matmul", pu[:, :],
                            wtU[sl][:, k * 128:(k + 1) * 128],
                            AR_X2[:, k * D + nt * 512:k * D + (nt + 1) * 512],
                            start=(k == 0), stop=(k == DC - 1),
                            inc=([(c.tM, 1)] if k == DC - 1 else None))
            ig, pg, fg = c.acc.acquire()
            em.wait_evt("tensor", fg)
            em.wait_evt("tensor", gev)
            for k in range(DC):
                mmg = em.op("tensor", "matmul", pg[:, :],
                            wtG[sl][:, k * 128:(k + 1) * 128],
                            AR_X2[:, k * D + nt * 512:k * D + (nt + 1) * 512],
                            start=(k == 0), stop=(k == DC - 1),
                            inc=([(c.tM, 1)] if k == DC - 1 else None))
            em.wait_evt("scalar", mmg)
            em.wait_evt("scalar", scr_free[0])
            em.wait_evt("scalar", vv2)
            sg = em.op("scalar", "activation", scr[0][:, :], pg[:, :],
                       AF.Silu, scale=wsg64[:, cc:cc + 1], inc=[(c.sA, 1)])
            c.acc.release(ig, sg)
            em.wait_evt("vector", mmu)
            em.wait_evt("vector", vv1)
            em.wait_evt("vector", scr_free[1])
            us = em.op("vector", "tensor_scalar", scr[1][:, :], pu[:, :],
                       wsu_su[:, cc:cc + 1], None, ALU.mult, inc=[(c.vV, 1)])
            c.acc.release(iu, us)
            p = nt % 4
            em.wait_evt("vector", sg)
            em.wait_evt("vector", hp_free[p])
            he = em.op("vector", "tensor_tensor", piece[p][:, :],
                       scr[0][:, :], scr[1][:, :], ALU.mult, inc=[(c.vV, 1)])
            scr_free[0] = he
            scr_free[1] = he
            em.wait_evt("sync", he)
            hp_free[p] = em.op(
                "sync", "dma_start",
                h_scr[nt * INTER + cc * 128:nt * INTER + (cc + 1) * 128, :],
                piece[p][:, :], inc=[(c.dS, 16)])
        wtU_free[sl] = mmu
        wtG_free[sl] = mmg
        if cc + 1 < IC:
            uev, gev = nxt_uev, nxt_gev
        if cc % 4 == 0:
            prep_part1(cc // 4)
        elif cc % 4 == 3:
            prep_part2(cc // 4)

    b2_done = mmg
    wd_prep_done = (c.dS, c.dS.n)

    em.scope = "B3"
    hsup = AR_H
    wdT = [AR_X[:, i * INTER:(i + 1) * INTER] for i in range(2)]
    Y = AR_X2[:, 0:SEG]
    x1p = [AR_X2[:, SEG + i * 512:SEG + (i + 1) * 512] for i in range(2)]
    prep_t_done = (c.tM, c.tM.n)
    hsup_free = prep_t_done
    wdT_free = [prep_t_done, prep_t_done]
    Y_free = b2_done
    x1p_free = [b2_done, b2_done]
    sq16 = AR_X2[:, SEG + 1024:SEG + 1024 + 8192]
    sq16_free = b2_done
    bev = None
    for sup in range(NT):
        em.wait("sync", *wd_prep_done)
        em.wait_evt("sync", hsup_free)
        hld = em.op("sync", "dma_start",
                    hsup[:, :].rearrange("p (cc n) -> p cc n", cc=IC),
                    h_scr[sup * INTER:(sup + 1) * INTER, :]
                    .rearrange("(cc p) n -> p cc n", p=128),
                    inc=[(c.dS, 16)])
        sq_evts = []
        for s in range(DC):
            sl = s % 2
            em.wait_evt("sync", wdT_free[sl])
            wld = em.op("sync", "dma_start",
                        wdT[sl].rearrange("i (cc o) -> i cc o", cc=IC),
                        wdT_scr[s * INTER:(s + 1) * INTER, :]
                        .rearrange("(cc i) o -> i cc o", i=128),
                        inc=[(c.dS, 16)])
            em.wait_evt("gpsimd", x1p_free[sl])
            x1ld = em.op("gpsimd", "dma_start", x1p[sl],
                         x1[s * 128:(s + 1) * 128,
                            sup * 512:(sup + 1) * 512], inc=[(c.dG, 16)])
            ia, acc, fr = c.acc.acquire()
            em.wait_evt("tensor", fr)
            em.wait_evt("tensor", wld)
            em.wait_evt("tensor", hld)
            for cci in range(IC):
                mme = em.op("tensor", "matmul", acc[:, :],
                            wdT[sl][:, cci * 128:(cci + 1) * 128],
                            hsup[:, cci * 512:(cci + 1) * 512],
                            start=(cci == 0), stop=(cci == IC - 1),
                            inc=([(c.tM, 1)] if cci == IC - 1 else None))
            wdT_free[sl] = mme
            em.wait_evt("vector", mme)
            em.wait_evt("vector", x1ld)
            em.wait_evt("vector", Y_free)
            em.wait_evt("vector", vv3)
            ye = em.op("vector", "scalar_tensor_tensor",
                       Y[:, s * 512:(s + 1) * 512], acc[:, :],
                       wsd64[:, s:s + 1], x1p[sl], ALU.mult, ALU.add,
                       inc=[(c.vV, 1)])
            c.acc.release(ia, ye)
            x1p_free[sl] = ye
            sq_evts.append(emit_sq(c, Y[:, s * 512:(s + 1) * 512], ye,
                                   sq16[:, s * 512:(s + 1) * 512], sq16_free))
        hsup_free = mme
        stats_evt = emit_stats_deferred(c, Y, ye, sq16, sq_evts, None)
        sq16_free = stats_evt
        A_t, B_t = emit_ln_finalize(c, ln_sc, stats_evt)

        def writer(s, s0, sup=sup):
            p = s % 2
            em.wait_evt("vector", g.piece_free[p])
            pe = em.op("vector", "tensor_tensor", g.piece[p][:, :], s0[:, :],
                       B_t[1][:, :], ALU.add, inc=[(c.vV, 1)])
            em.wait_evt("sync", pe)
            de = em.op("sync", "dma_start",
                       x2r_out[s * 128:(s + 1) * 128,
                               sup * 512:(sup + 1) * 512],
                       g.piece[p][:, :], inc=[(c.dS, 16)])
            g.piece_free[p] = de
            return pe

        Y_free = emit_ln_apply(c, g, Y, writer, A_t, B_t)



def emit_block_C(c, g, dr):
    em = c.em
    x2r = dr["x2r"]
    Wq, Wk, Wv, Wo = dr["Wq1"], dr["Wk1"], dr["Wv1"], dr["Wo1"]
    AMASK = dr["AMASK"]
    x3_out = dr["x3"]
    q_scr, k_scr, o_scr = dr["q1_scr"], dr["k1_scr"], dr["o1_scr"]

    AR_W = g.G1
    AR_V = g.G2
    xblk = [g.G3[:, 0:SEG], g.G3[:, SEG:2 * SEG]]
    nat1 = [g.G4[:, 0:2048]]
    nat3 = [g.G2[:, 512 + i * 2048:512 + (i + 1) * 2048] for i in range(2)]
    S_sb = g.S_sb
    amask = g.amask
    nmax = g.rowb[:, 0:1]
    rsum = g.rowb[:, 8:9]
    rinv = g.rowb[:, 16:17]
    scr = g.scr
    piece = g.piece
    ln_sc = g.ln
    sq = AR_V[:, 0:512]
    scr_free = g.scr_free
    piece_free = g.piece_free

    xblk_free = [None, None]
    nat_free1 = [None]
    amask_ev = em.op("sync", "dma_start", amask, AMASK[:, :],
                     inc=[(c.dS, 16)])
    x_re = x2r[:, :].rearrange("(c p) n -> p c n", p=128)

    def load_xblk(nt, slot):
        em.wait_evt("gpsimd", xblk_free[slot])
        src = x_re[:, :, nt * 512:(nt + 1) * 512]
        dst = xblk[slot].rearrange("p (c n) -> p c n", c=DC)
        return em.op("gpsimd", "dma_start", dst, src, inc=[(c.dG, 16)])

    em.scope = "C1"
    em.wait_evt("tensor", c.const_evt)

    def proj_plain(wt_view, slab_ready, spill, scale=None):
        pi = [0]
        mme = None
        for nt in range(NT):
            lev = load_xblk(nt, nt % 2)
            for o in range(DC):
                ia, acc, fr = c.acc.acquire()
                em.wait_evt("tensor", fr)
                em.wait_evt("tensor", slab_ready[o])
                em.wait_evt("tensor", lev)
                for k in range(DC):
                    mme = em.op(
                        "tensor", "matmul", acc[:, :],
                        wt_view[:, o * D + k * 128:o * D + (k + 1) * 128],
                        xblk[nt % 2][:, k * 512:(k + 1) * 512],
                        start=(k == 0), stop=(k == DC - 1),
                        inc=([(c.tM, 1)] if k == DC - 1 else None))
                p = pi[0] % 2
                em.wait_evt("vector", mme)
                em.wait_evt("vector", piece_free[p])
                if scale is None:
                    pe = em.op("vector", "tensor_copy", piece[p][:, :],
                               acc[:, :], inc=[(c.vV, 1)])
                else:
                    pe = em.op("vector", "tensor_scalar", piece[p][:, :],
                               acc[:, :], scale, None, ALU.mult,
                               inc=[(c.vV, 1)])
                c.acc.release(ia, pe)
                em.wait_evt("sync", pe)
                de = em.op("sync", "dma_start",
                           spill[nt * D + o * 128:nt * D + (o + 1) * 128, :],
                           piece[p][:, :], inc=[(c.dS, 16)])
                piece_free[p] = de
                pi[0] += 1
            xblk_free[nt % 2] = mme
        return mme

    wt_free = None
    sr = emit_wt_load(c, Wq, AR_W, 0, nat1, nat_free1, DC, D, wt_free)
    wt_free = proj_plain(AR_W, sr, q_scr, scale=ISQ)
    sr = emit_wt_load(c, Wk, AR_W, 0, nat1, nat_free1, DC, D, wt_free)
    wt_free = proj_plain(AR_W, sr, k_scr)
    sr = emit_wt_load(c, Wv, AR_W, 0, nat1, nat_free1, DC, D, wt_free)

    wtv_re = AR_W[:, :].rearrange("p (s x) -> p s x", s=DC)
    mme = None
    for m in range(4):
        for nb in range(NT):
            lev = load_xblk(nb, nb % 2)
            for ns in range(4):
                nch = nb * 4 + ns
                ia, acc, fr = c.acc.acquire()
                em.wait_evt("tensor", fr)
                em.wait_evt("tensor", sr[4 * m + 3])
                em.wait_evt("tensor", lev)
                for k in range(DC):
                    rhs = wtv_re[:, 4 * m:4 * m + 4, k * 128:(k + 1) * 128]
                    mme = em.op(
                        "tensor", "matmul", acc[:, :],
                        xblk[nb % 2][:, k * 512 + ns * 128:k * 512 + (ns + 1) * 128],
                        rhs, start=(k == 0), stop=(k == DC - 1),
                        inc=([(c.tM, 1)] if k == DC - 1 else None))
                em.wait_evt("scalar", mme)
                ve = em.op("scalar", "activation",
                           AR_V[:, nch * D + m * 512:nch * D + (m + 1) * 512],
                           acc[:, :], AF.Copy, inc=[(c.sA, 1)])
                c.acc.release(ia, ve)
            xblk_free[nb % 2] = mme
    V_ready = (c.sA, c.sA.n)
    v_mms_done = (c.tM, c.tM.n)
    c1_spill_done = (c.dS, c.dS.n)

    em.scope = "C2"
    kblk = [AR_W[:, u * SEG:(u + 1) * SEG] for u in range(4)]
    qblk = xblk[0]
    PT_sb = g.G3[:, SEG:2 * SEG]
    PT3 = PT_sb.rearrange("p (j q) -> p j q", j=DC)
    P4 = [g.G4[:, m2 * 2048:(m2 + 1) * 2048] for m2 in range(4)]
    nmax4 = [g.rowb[:, m2:m2 + 1] for m2 in range(4)]
    rsum4 = [g.rowb[:, 8 + m2:9 + m2] for m2 in range(4)]
    rinv4 = [g.rowb[:, 16 + m2:17 + m2] for m2 in range(4)]

    kblk_ld = [None] * 4
    qblk_free = xblk_free[0]
    pt_free = xblk_free[1]
    P4_free = [v_mms_done] * 4
    pb_free = list(c.acc.free)
    pc_free = c.pstat_free[1]
    op_free = [piece_free[0], piece_free[1]]
    opi = 0
    for t in range(4):
        em.wait("sync", *c1_spill_done)
        em.wait_evt("sync", v_mms_done)
        em.wait_evt("sync", qblk_free)
        qlev = em.op("sync", "dma_start",
                     qblk.rearrange("p (c n) -> p c n", c=DC),
                     q_scr[t * D:(t + 1) * D, :]
                     .rearrange("(c p) n -> p c n", p=128),
                     inc=[(c.dS, 16)])
        kblk_ld[t] = em.op("sync", "dma_start",
                           kblk[t].rearrange("p (c n) -> p c n", c=DC),
                           k_scr[t * D:(t + 1) * D, :]
                           .rearrange("(c p) n -> p c n", p=128),
                           inc=[(c.dS, 16)])
        for h in range(16):
            pn4 = [None] * 4
            for m2 in range(4):
                lhsq = qblk[:, h * 512 + m2 * 128:h * 512 + (m2 + 1) * 128]
                W = t * 512 + (m2 + 1) * 128
                for u in range(t):
                    em.wait_evt("tensor", pb_free[u])
                    em.wait_evt("tensor", qlev)
                    em.wait_evt("tensor", kblk_ld[u])
                    em.op("tensor", "matmul", c.PB[:, u * 512:(u + 1) * 512],
                          lhsq, kblk[u][:, h * 512:h * 512 + 512],
                          start=True, stop=True)
                em.wait_evt("tensor", pb_free[t])
                em.wait_evt("tensor", qlev)
                em.wait_evt("tensor", kblk_ld[t])
                mme = em.op("tensor", "matmul", c.PB[:, t * 512:W], lhsq,
                            kblk[t][:, h * 512:h * 512 + (m2 + 1) * 128],
                            start=True, stop=True, inc=[(c.tM, 1)])
                em.wait_evt("vector", mme)
                em.wait_evt("vector", amask_ev)
                em.op("vector", "tensor_tensor",
                      c.PB[:, t * 512 + m2 * 128:W],
                      c.PB[:, t * 512 + m2 * 128:W], amask, ALU.add)
                nw = em.op("vector", "reduce_max", nmax4[m2], c.PB[:, 0:W],
                           axis=AX.X, negate=True, inc=[(c.vV, 1)])
                em.wait_evt("scalar", nw)
                em.wait_evt("scalar", P4_free[m2])
                ex = em.op("scalar", "activation", P4[m2][:, 0:W],
                           c.PB[:, 0:W], AF.Exp, bias=nmax4[m2],
                           accum_out=rsum4[m2], inc=[(c.sA, 1)])
                for u in range(t + 1):
                    pb_free[u] = ex
                em.wait_evt("vector", ex)
                rie = em.op("vector", "reciprocal", rinv4[m2], rsum4[m2],
                            inc=[(c.vV, 1)])
                em.wait_evt("scalar", rie)
                pn4[m2] = em.op("scalar", "activation", P4[m2][:, 0:W],
                                P4[m2][:, 0:W], AF.Copy, scale=rinv4[m2],
                                inc=[(c.sA, 1)])
            ptw = None
            for m2 in range(4):
                ntr = 4 * t + m2 + 1
                tev = None
                for gg in range((ntr + 3) // 4):
                    gn = min(4, ntr - gg * 4)
                    bi, bt, bfree = c.ptr.acquire()
                    em.wait_evt("tensor", pn4[m2])
                    em.wait_evt("tensor", bfree)
                    for i in range(gn):
                        j2 = gg * 4 + i
                        inc = [(c.tM, 1)] if i == gn - 1 else None
                        tev = em.op("tensor", "transpose",
                                    bt[:, i * 128:(i + 1) * 128],
                                    P4[m2][:, j2 * 128:(j2 + 1) * 128],
                                    c.idn[:, :], inc=inc)
                    em.wait_evt("scalar", tev)
                    em.wait_evt("scalar", pt_free)
                    ptv = em.op("scalar", "activation",
                                PT3[:, gg * 4:gg * 4 + gn,
                                    m2 * 128:(m2 + 1) * 128],
                                bt[:, 0:gn * 128].rearrange(
                                    "p (j q) -> p j q", j=gn),
                                AF.Copy, inc=[(c.sA, 1)])
                    c.ptr.release(bi, ptv)
                P4_free[m2] = tev
                ptv2 = ptv
                for j2 in range(ntr, 4 * t + 4):
                    em.wait_evt("vector", pt_free)
                    ptv2 = em.op("vector", "memset",
                                 PT3[:, j2:j2 + 1, m2 * 128:(m2 + 1) * 128],
                                 0.0, inc=[(c.vV, 1)])
                ptw = ptv2
            em.wait_evt("tensor", pc_free)
            em.wait_evt("tensor", V_ready)
            em.wait_evt("tensor", ptw)
            for j2 in range(4 * t + 4):
                mme = em.op("tensor", "matmul", c.PC[:, 0:512],
                            AR_V[:, j2 * D + h * 128:j2 * D + (h + 1) * 128],
                            PT_sb[:, j2 * 512:(j2 + 1) * 512],
                            start=(j2 == 0), stop=(j2 == 4 * t + 3),
                            inc=([(c.tM, 1)] if j2 == 4 * t + 3 else None))
            pt_free = mme
            p = opi % 2
            em.wait_evt("vector", mme)
            em.wait_evt("vector", op_free[p])
            oe = em.op("vector", "tensor_copy", piece[p][:, :], c.PC[:, 0:512],
                       inc=[(c.vV, 1)])
            pc_free = oe
            em.wait_evt("sync", oe)
            op_free[p] = em.op("sync", "dma_start",
                               o_scr[t * D + h * 128:t * D + (h + 1) * 128, :],
                               piece[p][:, :], inc=[(c.dS, 16)])
            opi += 1
        qblk_free = mme

        qblk_free = mme

    piece_free[0] = op_free[0]
    piece_free[1] = op_free[1]
    for u in range(4):
        c.acc.free[u] = pb_free[u]
    c2_done = (c.dS, c.dS.n)

    em.scope = "C3"
    nat_free3 = [pt_free, pt_free]
    sr = emit_wt_load(c, Wo, AR_W, 0, nat3, nat_free3, DC, D, mme)
    ogblk = xblk[0]
    xres = xblk[1]
    Y = g.G4[:, 0:SEG]
    sq16 = AR_V[:, 8192:16384]
    sq16_free = pt_free
    ogblk_free = qblk_free
    xres_free = pt_free
    Y_free = P4_free[3]
    for nt in range(NT):
        em.wait("sync", *c2_done)
        em.wait_evt("sync", ogblk_free)
        olev = em.op("sync", "dma_start",
                     ogblk.rearrange("p (c n) -> p c n", c=DC),
                     o_scr[nt * D:(nt + 1) * D, :]
                     .rearrange("(c p) n -> p c n", p=128),
                     inc=[(c.dS, 16)])
        em.wait_evt("gpsimd", xres_free)
        xlev = em.op("gpsimd", "dma_start",
                     xres.rearrange("p (cc n) -> p cc n", cc=DC),
                     x_re[:, :, nt * 512:(nt + 1) * 512], inc=[(c.dG, 16)])
        sq_evts = []
        for o in range(DC):
            ia, acc, fr = c.acc.acquire()
            em.wait_evt("tensor", fr)
            em.wait_evt("tensor", sr[o])
            em.wait_evt("tensor", olev)
            for k in range(DC):
                mme = em.op("tensor", "matmul", acc[:, :],
                            AR_W[:, o * D + k * 128:o * D + (k + 1) * 128],
                            ogblk[:, k * 512:(k + 1) * 512],
                            start=(k == 0), stop=(k == DC - 1),
                            inc=([(c.tM, 1)] if k == DC - 1 else None))
            em.wait_evt("vector", mme)
            em.wait_evt("vector", xlev)
            em.wait_evt("vector", Y_free)
            ye = em.op("vector", "tensor_tensor", Y[:, o * 512:(o + 1) * 512],
                       acc[:, :], xres[:, o * 512:(o + 1) * 512], ALU.add,
                       inc=[(c.vV, 1)])
            c.acc.release(ia, ye)
            sq_evts.append(emit_sq(c, Y[:, o * 512:(o + 1) * 512], ye,
                                   sq16[:, o * 512:(o + 1) * 512], sq16_free))
        ogblk_free = mme
        xres_free = ye
        stats_evt = emit_stats_deferred(c, Y, ye, sq16, sq_evts, None)
        sq16_free = stats_evt
        A_t, B_t = emit_ln_finalize(c, ln_sc, stats_evt)

        def writer(o, s0, nt=nt):
            p = o % 2
            em.wait_evt("vector", piece_free[p])
            pe = em.op("vector", "tensor_tensor", piece[p][:, :], s0[:, :],
                       B_t[1][:, :], ALU.add, inc=[(c.vV, 1)])
            em.wait_evt("sync", pe)
            de = em.op("sync", "dma_start",
                       x3_out[o * 128:(o + 1) * 128, nt * 512:(nt + 1) * 512],
                       piece[p][:, :], inc=[(c.dS, 16)])
            piece_free[p] = de
            return pe

        Y_free = emit_ln_apply(c, g, Y, writer, A_t, B_t)



def emit_block_D(c, g, dr):
    em = c.em
    x3 = dr["x3"]
    Wup, Wdn = dr["Wup1"], dr["Wdn1"]
    UPS, DNS, OSC = dr["UPS"], dr["DNS"], dr["OSC"]
    yT_out = dr["yT"]
    hid_scr, wdT_scr = dr["hid_scr"], dr["wd1T_scr"]

    AR_A = g.G1
    AR_B2 = g.G3
    upsc = g.vecsB[:, 0:IC]
    dns_sb = g.vecsD[:, 0:DC]
    dsos = g.vecsD[:, DC:2 * DC]
    osc_sb = g.vecsD[:, 2 * DC:2 * DC + 1]
    scr = g.scr
    piece = g.piece
    fpiece = [g.FA[:, 2560 + i * 512:2560 + (i + 1) * 512] for i in range(2)]
    sq = g.G2[:, 0:512]
    ln_sc = g.ln
    scr_free = g.scr_free
    piece_free = g.piece_free
    fpiece_free = [None, None]

    for v_sb, v_d in ((upsc, UPS), (dns_sb, DNS), (osc_sb, OSC)):
        em.op("sync", "dma_start", v_sb, v_d[:, :], inc=[(c.dS, 16)])
    em.wait("vector", c.dS, c.dS.n)
    vv1 = em.op("vector", "tensor_scalar", dsos, dns_sb, osc_sb, None,
                ALU.mult, inc=[(c.vV, 1)])

    x3_re = x3[:, :].rearrange("(c p) n -> p c n", p=128)
    x3ld = em.op("gpsimd", "dma_start",
                 AR_A[:, :].rearrange("p (c n) -> p c n", c=DC),
                 x3_re[:, :, :], inc=[(c.dG, 16)])

    em.scope = "D1"
    natU = [AR_B2[:, i * D:(i + 1) * D] for i in range(2)]
    natS = [AR_B2[:, 2 * D:3 * D], AR_B2[:, 5 * D:6 * D]]
    wtU = [AR_B2[:, (3 + i) * D:(4 + i) * D] for i in range(2)]
    natU_free = [None, None]
    natS_free = [None, None]
    wtU_free = [None, None]
    pfree = [None, None, None, None]

    nat8 = g.G4[:, 0:INTER]
    nat8_free = [None]
    prep_sg = [None]

    def prep_D_load(s):
        em.wait_evt("gpsimd", nat8_free[0])
        ld = em.op("gpsimd", "dma_start", nat8,
                   Wdn[s * 128:(s + 1) * 128, :], inc=[(c.dG, 16)])
        em.wait_evt("scalar", ld)
        prep_sg[0] = em.op("scalar", "activation", nat8, nat8, AF.Sign,
                           inc=[(c.sA, 1)])

    def prep_slab_D(s):
        sg = prep_sg[0]
        tev = None
        for gg in range(16):
            bi, bt, bfree = c.ptr.acquire()
            em.wait_evt("tensor", sg)
            em.wait_evt("tensor", bfree)
            for i in range(4):
                k = gg * 4 + i
                inc = [(c.tM, 1)] if i == 3 else None
                tev = em.op("tensor", "transpose",
                            bt[:, i * 128:(i + 1) * 128],
                            nat8[:, k * 128:(k + 1) * 128],
                            c.idn[:, :], inc=inc)
            p = gg % 2
            em.wait_evt("vector", tev)
            em.wait_evt("vector", pfree[p])
            ev = em.op("vector", "tensor_copy", piece[p][:, :], bt[:, 0:512],
                       inc=[(c.vV, 1)])
            c.ptr.release(bi, ev)
            em.wait_evt("sync", ev)
            dst = wdT_scr[s * INTER + gg * 512:s * INTER + (gg + 1) * 512, :]
            dst3 = dst.rearrange("(cc i) o -> i cc o", i=128)
            src3 = piece[p][:, :].rearrange("i (cc o) -> i cc o", cc=4)
            pfree[p] = em.op("sync", "dma_start", dst3, src3,
                             inc=[(c.dS, 16)])
        nat8_free[0] = tev

    em.wait_evt("tensor", c.const_evt)
    mme = None
    evac_by_cc = {}

    def d1_slab(cc2):
        sl2 = cc2 % 2
        em.wait_evt("gpsimd", natU_free[sl2])
        ld = em.op("gpsimd", "dma_start", natU[sl2],
                   Wup[cc2 * 128:(cc2 + 1) * 128, :], inc=[(c.dG, 16)])
        em.wait_evt("scalar", ld)
        em.wait_evt("scalar", natS_free[sl2])
        sg = em.op("scalar", "activation", natS[sl2], natU[sl2], AF.Sign,
                   inc=[(c.sA, 1)])
        natU_free[sl2] = sg
        tev = None
        evac = None
        for gg in range(4):
            bi, bt, bfree = c.ptr.acquire()
            em.wait_evt("tensor", sg)
            em.wait_evt("tensor", bfree)
            for i in range(4):
                k = gg * 4 + i
                inc = [(c.tM, 1)] if i == 3 else None
                tev = em.op("tensor", "transpose",
                            bt[:, i * 128:(i + 1) * 128],
                            natS[sl2][:, k * 128:(k + 1) * 128],
                            c.idn[:, :], inc=inc)
            em.wait_evt("vector", tev)
            em.wait_evt("vector", wtU_free[sl2])
            evac = em.op("vector", "tensor_copy",
                         wtU[sl2][:, gg * 512:(gg + 1) * 512],
                         bt[:, 0:512], inc=[(c.vV, 1)])
            c.ptr.release(bi, evac)
        natS_free[sl2] = tev
        evac_by_cc[cc2] = evac

    d1_slab(0)
    for cc in range(IC):
        sl = cc % 2
        if cc + 1 < IC:
            d1_slab(cc + 1)
        evac = evac_by_cc.pop(cc)
        for nt in range(NT):
            ia, acc, fr = c.acc.acquire()
            em.wait_evt("tensor", fr)
            em.wait_evt("tensor", evac)
            em.wait_evt("tensor", x3ld)
            for k in range(DC):
                mme = em.op("tensor", "matmul", acc[:, :],
                            wtU[sl][:, k * 128:(k + 1) * 128],
                            AR_A[:, k * D + nt * 512:k * D + (nt + 1) * 512],
                            start=(k == 0), stop=(k == DC - 1),
                            inc=([(c.tM, 1)] if k == DC - 1 else None))
            p = nt % 4
            em.wait_evt("scalar", mme)
            em.wait_evt("scalar", pfree[p])
            he = em.op("scalar", "activation", piece[p][:, :], acc[:, :],
                       AF.Relu, scale=upsc[:, cc:cc + 1], inc=[(c.sA, 1)])
            c.acc.release(ia, he)
            em.wait_evt("sync", he)
            pfree[p] = em.op(
                "sync", "dma_start",
                hid_scr[nt * INTER + cc * 128:nt * INTER + (cc + 1) * 128, :],
                piece[p][:, :], inc=[(c.dS, 16)])
        wtU_free[sl] = mme
        if cc % 4 == 1:
            prep_D_load(cc // 4)
        elif cc % 4 == 3:
            prep_slab_D(cc // 4)
    d1_mms_done = (c.tM, c.tM.n)
    wd_prep_done = (c.dS, c.dS.n)

    em.scope = "D2"
    hsup = AR_A
    wdT = [AR_B2[:, i * INTER:(i + 1) * INTER] for i in range(2)]
    Y = g.G4[:, 0:SEG]
    x3p = g.x3p
    hsup_free = d1_mms_done
    wdT_free = [d1_mms_done, d1_mms_done]
    sq16 = g.G2[:, 0:8192]
    sq16_free = None
    Y_free = None
    x3p_free = [None, None]
    for sup in range(NT):
        em.wait("sync", *wd_prep_done)
        em.wait_evt("sync", hsup_free)
        hld = em.op("sync", "dma_start",
                    hsup[:, :].rearrange("p (cc n) -> p cc n", cc=IC),
                    hid_scr[sup * INTER:(sup + 1) * INTER, :]
                    .rearrange("(cc p) n -> p cc n", p=128),
                    inc=[(c.dS, 16)])
        sq_evts = []
        for s in range(DC):
            sl = s % 2
            em.wait_evt("sync", wdT_free[sl])
            wld = em.op("sync", "dma_start",
                        wdT[sl].rearrange("i (cc o) -> i cc o", cc=IC),
                        wdT_scr[s * INTER:(s + 1) * INTER, :]
                        .rearrange("(cc i) o -> i cc o", i=128),
                        inc=[(c.dS, 16)])
            em.wait_evt("gpsimd", x3p_free[sl])
            x3ld2 = em.op("gpsimd", "dma_start", x3p[sl],
                          x3[s * 128:(s + 1) * 128,
                             sup * 512:(sup + 1) * 512], inc=[(c.dG, 16)])
            ia, acc, fr = c.acc.acquire()
            em.wait_evt("tensor", fr)
            em.wait_evt("tensor", wld)
            em.wait_evt("tensor", hld)
            for cci in range(IC):
                mme = em.op("tensor", "matmul", acc[:, :],
                            wdT[sl][:, cci * 128:(cci + 1) * 128],
                            hsup[:, cci * 512:(cci + 1) * 512],
                            start=(cci == 0), stop=(cci == IC - 1),
                            inc=([(c.tM, 1)] if cci == IC - 1 else None))
            wdT_free[sl] = mme
            em.wait_evt("vector", mme)
            em.wait_evt("vector", x3ld2)
            em.wait_evt("vector", Y_free)
            em.wait_evt("vector", vv1)
            ye = em.op("vector", "scalar_tensor_tensor",
                       Y[:, s * 512:(s + 1) * 512], acc[:, :],
                       dsos[:, s:s + 1], x3p[sl], ALU.mult, ALU.add,
                       inc=[(c.vV, 1)])
            c.acc.release(ia, ye)
            x3p_free[sl] = ye
            sq_evts.append(emit_sq(c, Y[:, s * 512:(s + 1) * 512], ye,
                                   sq16[:, s * 512:(s + 1) * 512], sq16_free))
        hsup_free = mme
        stats_evt = emit_stats_deferred(c, Y, ye, sq16, sq_evts, None)
        sq16_free = stats_evt
        A_t, B_t = emit_ln_finalize(c, ln_sc, stats_evt)

        def writer(s, s0, sup=sup):
            p = s % 2
            em.wait_evt("vector", fpiece_free[p])
            pe = em.op("vector", "tensor_tensor", fpiece[p][:, :], s0[:, :],
                       B_t[1][:, :], ALU.add, inc=[(c.vV, 1)])
            em.wait_evt("sync", pe)
            de = em.op("sync", "dma_start",
                       yT_out[s * 128:(s + 1) * 128,
                              sup * 512:(sup + 1) * 512],
                       fpiece[p][:, :], inc=[(c.dS, 16)])
            fpiece_free[p] = de
            return pe

        Y_free = emit_ln_apply(c, g, Y, writer, A_t, B_t)



def build_full():
    nc = bass.Bass("TRN2", target_bir_lowering=False, debug=False)
    stack = contextlib.ExitStack()
    dr = {}
    dr["xT"] = nc.dram_tensor("xT", [D, T], F32, kind="ExternalInput")
    for nm in ("Wq", "Wk", "Wv", "Wg", "Wp"):
        dr[nm] = nc.dram_tensor(nm, [D, D], F32, kind="ExternalInput")
    dr["LMASK"] = nc.dram_tensor("LMASK", [4 * 128, 512], BF16,
                                 kind="ExternalInput")
    dr["Wup"] = nc.dram_tensor("Wup", [INTER, D], F32, kind="ExternalInput")
    dr["Wgt"] = nc.dram_tensor("Wgt", [INTER, D], F32, kind="ExternalInput")
    dr["Wdn"] = nc.dram_tensor("Wdn", [D, INTER], F32, kind="ExternalInput")
    dr["H2048"] = nc.dram_tensor("H2048", [D, D], BF16, kind="ExternalInput")
    dr["H128"] = nc.dram_tensor("H128", [128, 128], BF16, kind="ExternalInput")
    dr["SUU"] = nc.dram_tensor("SUU", [128, DC], F32, kind="ExternalInput")
    dr["SUD"] = nc.dram_tensor("SUD", [128, IC], F32, kind="ExternalInput")
    dr["WSU"] = nc.dram_tensor("WSU", [128, IC], F32, kind="ExternalInput")
    dr["WSG"] = nc.dram_tensor("WSG", [128, IC], F32, kind="ExternalInput")
    dr["WSD"] = nc.dram_tensor("WSD", [128, DC], F32, kind="ExternalInput")
    for nm in ("Wq1", "Wk1", "Wv1", "Wo1"):
        dr[nm] = nc.dram_tensor(nm, [D, D], F32, kind="ExternalInput")
    dr["AMASK"] = nc.dram_tensor("AMASK", [128, 128], BF16,
                                 kind="ExternalInput")
    dr["Wup1"] = nc.dram_tensor("Wup1", [INTER, D], F32, kind="ExternalInput")
    dr["Wdn1"] = nc.dram_tensor("Wdn1", [D, INTER], F32, kind="ExternalInput")
    dr["UPS"] = nc.dram_tensor("UPS", [128, IC], F32, kind="ExternalInput")
    dr["DNS"] = nc.dram_tensor("DNS", [128, DC], F32, kind="ExternalInput")
    dr["OSC"] = nc.dram_tensor("OSC", [128, 1], F32, kind="ExternalInput")
    dr["yT"] = nc.dram_tensor("yT", [D, T], F32, kind="ExternalOutput")
    dr["x1"] = nc.dram_tensor("x1", [D, T], BF16, kind="Internal")
    dr["x2r"] = nc.dram_tensor("x2r", [D, T], BF16, kind="Internal")
    dr["x3"] = nc.dram_tensor("x3", [D, T], BF16, kind="Internal")
    for nm in ("q_scr", "k_scr", "g_scr", "og_scr"):
        dr[nm] = nc.dram_tensor(nm, [4 * D, 512], BF16, kind="Internal")
    dr["r_scr"] = nc.dram_tensor("r_scr", [1, T], F32, kind="Internal")
    dr["h_scr"] = nc.dram_tensor("h_scr", [4 * INTER, 512], BF16,
                                 kind="Internal")
    dr["wdT_scr"] = nc.dram_tensor("wdT_scr", [DC * INTER, 128], BF16,
                                   kind="Internal")
    for nm in ("q1_scr", "k1_scr", "o1_scr"):
        dr[nm] = nc.dram_tensor(nm, [4 * D, 512], BF16, kind="Internal")
    dr["hid_scr"] = nc.dram_tensor("hid_scr", [4 * INTER, 512], BF16,
                                   kind="Internal")
    dr["wd1T_scr"] = nc.dram_tensor("wd1T_scr", [DC * INTER, 128], BF16,
                                    kind="Internal")

    with stack:
        c = make_ctx(nc, stack)
        g = alloc_arenas(nc, stack)
        emit_block_A(c, g, dr)
        barrier(c)
        g.scr_free = [None, None]
        g.piece_free = [None, None, None, None]
        emit_block_B(c, g, dr)
        barrier(c)
        g.scr_free = [None, None]
        g.piece_free = [None, None, None, None]
        emit_block_C(c, g, dr)
        barrier(c)
        g.scr_free = [None, None]
        g.piece_free = [None, None, None, None]
        emit_block_D(c, g, dr)
        c.em.wait("sync", c.dS, c.dS.n)
        c.em.wait("gpsimd", c.dG, c.dG.n)
        c.em.replay()
    return nc



def la_masks():
    m = np.zeros((4 * 128, 512), np.float32)
    kk = np.arange(128)[:, None]
    qq = np.arange(512)[None, :]
    for mi in range(4):
        m[mi * 128:(mi + 1) * 128] = (kk <= qq - 128 * mi)
    return m.astype(bfloat16)


def attn_mask():
    qq = np.arange(128)[:, None]
    kk = np.arange(128)[None, :]
    return np.where(kk <= qq, 0.0, -1e30).astype(np.float32).astype(bfloat16)


def hadamard(n):
    H = np.array([[1.0]], dtype=np.float32)
    while H.shape[0] < n:
        H = np.block([[H, H], [H, -H]]).astype(np.float32)
    return H


def pmajor(v, nchunks):
    return np.ascontiguousarray(np.asarray(v).reshape(nchunks, 128).T)


def kernel(**inputs):
    x = inputs["L0_attn_x"]
    n = 8
    shared = dict(const_inputs())
    shared["Wq"] = inputs["L0_attn_Wq"]
    shared["Wk"] = inputs["L0_attn_Wk"]
    shared["Wv"] = inputs["L0_attn_Wv"]
    shared["Wg"] = inputs["L0_attn_Wg"]
    shared["Wp"] = inputs["L0_attn_Wp"]
    shared["LMASK"] = la_masks()
    shared["Wup"] = inputs["L0_ffn_W_up"]
    shared["Wgt"] = inputs["L0_ffn_W_gate"]
    shared["Wdn"] = inputs["L0_ffn_W_down"]
    shared["H2048"] = hadamard(2048).astype(bfloat16)
    shared["H128"] = hadamard(128).astype(bfloat16)
    shared["SUU"] = pmajor(inputs["L0_ffn_SU_ug"], DC)
    shared["SUD"] = pmajor(inputs["L0_ffn_SU_dp"], IC)
    shared["WSU"] = pmajor(inputs["L0_ffn_Wscale_ug"][:INTER], IC)
    shared["WSG"] = pmajor(inputs["L0_ffn_Wscale_ug"][INTER:], IC)
    shared["WSD"] = pmajor(inputs["L0_ffn_Wscale_dp"], DC)
    shared["Wq1"] = inputs["L1_attn_Wq"]
    shared["Wk1"] = inputs["L1_attn_Wk"]
    shared["Wv1"] = inputs["L1_attn_Wv"]
    shared["Wo1"] = inputs["L1_attn_Wo"]
    shared["AMASK"] = attn_mask()
    shared["Wup1"] = inputs["L1_ffn_up_weight"]
    shared["Wdn1"] = inputs["L1_ffn_down_weight"]
    shared["UPS"] = pmajor(inputs["L1_ffn_up_scale"], IC)
    shared["DNS"] = pmajor(inputs["L1_ffn_down_scale"], DC)
    shared["OSC"] = np.ascontiguousarray(np.broadcast_to(
        np.asarray(inputs["L1_ffn_output_scale"], np.float32).reshape(1, 1),
        (128, 1)))

    in_maps = [{**shared, "xT": np.ascontiguousarray(x[b].T)}
               for b in range(n)]
    nc = build_full()
    res = run_bass_kernel_spmd(nc, in_maps, core_ids=list(range(n)))
    return np.stack([np.asarray(r["yT"], dtype=np.float32).T
                     for r in res.results])
